# version 13
# speedup vs baseline: 1.1373x; 1.0012x over previous
import math
import numpy as np
from contextlib import ExitStack

import concourse.bass as bass
import concourse.mybir as mybir
from concourse.alu_op_type import AluOpType
from concourse.bass_utils import run_bass_kernel_spmd
from concourse.library_config import attn as _gp_attn_lib

F32 = mybir.dt.float32
BF16 = mybir.dt.bfloat16
F16 = mybir.dt.float16
ACT = mybir.ActivationFunctionType
AX = mybir.AxisListType.X

B, S, D = 4, 4096, 1024
SQ = 2048
NSLOT = 16
NKB = 32
DC = 256
HA, DA = 16, 64
HB, DB = 8, 128
E, HID, GD = 8, 2048, 16
RWIN = 4224
RPITCH = RWIN + 1
POSW = 8192
N_SUB = 4
DEBUG = False
EPS = 1e-5
MASK_NEG = -60000.0

ENGINES = ("sync", "scalar", "vector", "tensor", "gpsimd")


def _positions_np():
    seq_len, feature_size = S, 192
    n = feature_size // 6
    distances = np.arange(-seq_len + 1, seq_len, dtype=np.float64)
    absd = np.abs(distances)[:, None]
    max_range = math.log(seq_len) / math.log(2.0)
    half_life = 2.0 ** np.linspace(3.0, max_range, n)
    feat_exp = np.exp(-math.log(2.0) / half_life[None, :] * absd)
    center_widths = 2.0 ** np.arange(1, n + 1, dtype=np.float64) - 1.0
    feat_cm = (center_widths[None, :] > absd).astype(np.float64)
    stddev = seq_len / (2.0 * n)
    start_mean = seq_len / n
    mean = np.linspace(start_mean, float(seq_len), n)[None, :]
    conc = (mean / stddev) ** 2
    rate = mean / stddev ** 2
    with np.errstate(divide="ignore", invalid="ignore"):
        log_unnorm = np.where(conc - 1.0 == 0.0, 0.0,
                              (conc - 1.0) * np.log(absd))
        log_unnorm = np.where(absd == 0.0,
                              np.where(conc - 1.0 > 0, -np.inf, 0.0),
                              log_unnorm) - rate * absd
    import math as _m
    _lg = np.frompyfunc(_m.lgamma, 1, 1)
    log_norm = _lg(conc).astype(np.float64) - conc * np.log(rate)
    probs = np.exp(log_unnorm - log_norm) + 1e-08
    feat_gamma = probs / probs.max(axis=-1, keepdims=True)
    emb = np.concatenate([feat_exp, feat_cm, feat_gamma], axis=-1)
    emb = np.concatenate([emb, np.sign(distances)[:, None] * emb], axis=-1)
    return emb.astype(np.float32)


def _pos_staged(par):
    pos = _positions_np().T
    out = np.zeros((192, POSW), np.float32)
    if par == 0:
        out[:, :8191] = pos
    else:
        out[:, 128:128 + 8063] = pos[:, :8063]
    return out


def _masks_staged(par):
    m = np.zeros((NSLOT, 128, 256), np.float32)
    qq = np.arange(128)
    kk = np.arange(128)
    for i in range(NSLOT):
        if par == 0:
            m[i, :, 0:128] = np.where(kk[None, :] > qq[:, None], MASK_NEG, 0.0)
            m[i, :, 128:256] = MASK_NEG
        else:
            m[i, :, 128:256] = np.where(kk[None, :] > qq[:, None], MASK_NEG, 0.0)
    return m


def _gath_global_block(c):
    return 2 * c if c < 16 else 2 * (c - 16) + 1


class Plan:
    def __init__(self):
        self.ev = {e: [] for e in ENGINES}

    def add(self, e, fn):
        self.ev[e].append(fn)


class Sem:
    __slots__ = ("name", "h", "n")

    def __init__(self, name):
        self.name = name
        self.h = None
        self.n = 0


def dma(pl, q, sem, out_ap, in_ap):
    def f(e, o=out_ap, i=in_ap, s=sem):
        e.dma_start(out=o, in_=i).then_inc(s.h, 16)
    pl.add(q, f)
    sem.n += 16
    return sem.n


def wait(pl, e, sem, val):
    pl.add(e, lambda eng, s=sem, v=val: eng.wait_ge(s.h, v))


def tick(pl, e, sem, fn):
    def g(eng, f=fn, s=sem):
        f(eng).then_inc(s.h, 1)
    pl.add(e, g)
    sem.n += 1
    return sem.n


def emit(pl, e, fn):
    pl.add(e, fn)


def barrier(pl, compute_sems, owner, queues):
    for e in ENGINES:
        for s_ in compute_sems:
            if owner.get(s_) != e:
                wait(pl, e, s_, s_.n)
        for qe, qsems in queues.items():
            for qsem in qsems:
                wait(pl, e, qsem, qsem.n)


def ceil_div(a, b):
    return (a + b - 1) // b


def col_chunks(total, step=512):
    out = []
    off = 0
    while off < total:
        w = min(step, total - off)
        out.append((off, w))
        off += w
    return out


RACECHECK = False

def build_program(n_sub=N_SUB, cc_groups=None):
    nc = bass.Bass("TRN2", target_bir_lowering=False,
                   detect_race_conditions=RACECHECK)
    pl = Plan()

    def din(name, shape, dt=F32):
        return nc.dram_tensor(name, shape, dt, kind="ExternalInput").ap()

    x_own = din("x_own", [SQ, D])
    x_full = din("x_full", [S, D])
    wdq = din("wdq", [D, DC]);   wuq = din("wuq", [DC, D])
    wdkv = din("wdkv", [D, DC]); wuk = din("wuk", [DC, D]); wuv = din("wuv", [DC, D])
    wo = din("wo", [D, D])
    wqs = din("wqs", [D, D]); wks = din("wks", [D, D]); wvs = din("wvs", [D, D])
    wout = din("wout", [D, D]); wrel = din("wrel", [192, D])
    rebm = din("rebm", [HB, DB]); rpbm = din("rpbm", [HB, DB])
    boutv = din("boutv", [D])
    wred = din("wred", [D, GD]); wgm = din("wgm", [E, GD])
    W1 = din("W1", [E, D, HID]); b1m = din("b1m", [E, HID])
    W2 = din("W2", [E, HID, D]); b2m = din("b2m", [E, D])
    identc = din("identc", [128, 128])
    maskc = din("maskc", [NSLOT, 128, 256])
    posc = din("posc", [192, POSW])

    yout = nc.dram_tensor("yout", [SQ, D], F32, kind="ExternalOutput").ap()
    if DEBUG:
        dbg_xres = nc.dram_tensor("dbg_xres", [SQ, D], F32,
                                  kind="ExternalOutput").ap()
        dbg_ccout = nc.dram_tensor("dbg_ccout", [2 * SQ, D], BF16,
                                   kind="ExternalOutput").ap()
        dbg_outb = nc.dram_tensor("dbg_outb", [HB, 128, SQ], BF16,
                                  kind="ExternalOutput").ap()
        dbg_p = nc.dram_tensor("dbg_p", [2, 128, 4096], BF16,
                               kind="ExternalOutput").ap()
        dbg_st = nc.dram_tensor("dbg_st", [2, 128, 64], F32,
                                kind="ExternalOutput").ap()
        dbg_hd = nc.dram_tensor("dbg_hd", [4, 128, S], BF16,
                                kind="ExternalOutput").ap()

    xres = nc.dram_tensor("xres", [SQ, D], F32).ap()
    relscr = nc.dram_tensor("relscr", [2, 128 * RPITCH + 512], F16).ap()
    outbsp = nc.dram_tensor("outbsp", [HB, 128, SQ], BF16).ap()
    ccin = nc.dram_tensor("ccin", [SQ, D], BF16)
    ccout = nc.dram_tensor("ccout", [2 * SQ, D], BF16)

    dW, dS, dG = Sem("dW"), Sem("dS"), Sem("dG")
    dX = [Sem("dX0"), Sem("dX1")]
    dB = [Sem("dB0"), Sem("dB1")]
    dO = [Sem("dO0"), Sem("dO1")]
    dN = [Sem("dN0"), Sem("dN1")]
    dRW = Sem("dRW")
    dD = [Sem("dD0"), Sem("dD1")]
    dW1 = [Sem("dW1a"), Sem("dW1b")]
    dW2 = Sem("dW2")
    dP = [Sem("dPa"), Sem("dPb")]
    dH = Sem("dH")
    dSp = [Sem("dSpa"), Sem("dSpb")]
    T, V, SC, GP, CCs, BARs = (Sem("T"), Sem("V"), Sem("SC"), Sem("GP"),
                               Sem("CC"), Sem("BAR"))
    ALLSEMS = [dW, dS, dG, dX[0], dX[1], dB[0], dB[1], dO[0], dO[1],
               dN[0], dN[1], dRW, dD[0], dD[1], dW1[0], dW1[1], dW2,
               dP[0], dP[1], dH, dSp[0], dSp[1], T, V, SC, GP, CCs, BARs]
    OWNER = {T: "tensor", V: "vector", SC: "scalar", GP: "gpsimd"}
    WM = {e: {} for e in ENGINES}
    queues = {"sync": [dW, dX[0], dX[1], dB[0], dB[1], dO[0], dO[1],
                       dN[0], dN[1], dRW],
              "scalar": [dS, dD[0], dD[1]],
              "gpsimd": [dG, dW1[0], dW1[1], dW2, dP[0], dP[1], dH,
                         dSp[0], dSp[1]]}

    def need(e, sem, val):
        if sem is None or val == "any" or val is None:
            return
        if isinstance(val, tuple):
            sem, val = val
        if val <= 0:
            return
        if OWNER.get(sem) == e:
            return
        if WM[e].get(sem, 0) >= val:
            return
        WM[e][sem] = val
        wait(pl, e, sem, val)

    def bar():
        barrier(pl, [T, V, SC, GP, CCs], OWNER, queues)
        for e in ENGINES:
            for s in ALLSEMS:
                if s is not BARs:
                    WM[e][s] = s.n

    es = ExitStack()
    CUR = [es]
    CURP = [es]

    def sb(name, shape, dt):
        return CUR[0].enter_context(nc.sbuf_tensor(name, shape, dt))

    def ps(name, shape):
        return CURP[0].enter_context(nc.psum_tensor(name, shape, F32))

    identB = sb("identB", [128, 128], BF16)
    dma(pl, "gpsimd", dG, identB[:, :], identc)
    identH = sb("identH", [128, 128], F16)
    ident_tgt = dma(pl, "gpsimd", dG, identH[:, :], identc)
    need("gpsimd", dG, ident_tgt)
    need("tensor", dG, ident_tgt)
    need("vector", dG, ident_tgt)
    konst = sb("konst", [128, 2], F32)

    def _k(e):
        e.memset(konst[:, 0:1], 0.0)
        return e.memset(konst[:, 1:2], EPS)
    konst_rdy = tick(pl, "vector", V, _k)

    uid = [0]

    def nm(p):
        uid[0] += 1
        return f"{p}{uid[0]}"

    def load_w(dst, src, R, C):
        for r in range(R // 128):
            dma(pl, "gpsimd", dG, dst[:, r * C:(r + 1) * C],
                src[r * 128:(r + 1) * 128, :])
        return dG.n

    def attn_slots(psS, psS_free, psT2, psAV, psL, scores_all, st,
                   lrow_sb, linvB, onesH, onesR, scale,
                   qk_fn, seg_evac_fn, av_lhs, av_out_slice, av_tilepos,
                   evac_fn, ext_fn, diag_rdy_fn=None, prep_fn=None,
                   seg_evac_eng=None, n_slots=NSLOT, av_head_hook=None):
        need("tensor", SC, SC.n)
        need("tensor", V, V.n)
        need("scalar", T, T.n)
        need("vector", T, T.n)
        psT_free = [0] * len(psT2)
        psAV_free = [(V, 0)] * 4
        psL_free = [(T, 0)] * 4
        bc_busy = [(V, 0)] * 4
        sc_busy = {k: 0 for k in range(4)}
        p_busy = {k: (T, 0) for k in range(4)}
        slot_info = {}

        def pslice(par, kb, nsl=1):
            v3 = scores_all[:, :].rearrange("p (s c) -> p s c", c=4096)
            return v3[:, par:par + nsl, kb * 128:(kb + 1) * 128]

        def qk_stage1(i):
            ext = ext_fn(i)
            ncols = ext * 128
            nsg = ceil_div(ncols, 1024)
            par = i % 4
            sb0 = par * 4
            sbase = par * 4096
            if diag_rdy_fn is not None:
                need("vector", *diag_rdy_fn(i))
                need("scalar", *diag_rdy_fn(i))
            need("vector", SC, sc_busy[par])
            need("vector", *p_busy[par])
            last_ev = None
            for sgi in range(nsg):
                koff = sgi * 1024
                scols = min(1024, ncols - koff)
                sl = sgi % 2
                if isinstance(psS_free[sl], tuple):
                    need("tensor", *psS_free[sl])
                qk_fn(i, koff, scols, psS[sl], sl)
                ts_ = T.n
                eng = seg_evac_eng(i, sgi) if seg_evac_eng else "vector"
                esem = V if eng == "vector" else SC
                need(eng, T, ts_)
                if eng == "scalar":
                    need(eng, SC, sc_busy[par])
                    need(eng, *p_busy[par])
                ev = tick(pl, eng, esem,
                          lambda e, i=i, sgi=sgi, koff=koff, scols=scols,
                          sl=sl, sbase=sbase:
                          seg_evac_fn(e, i, sgi, koff, scols, psS[sl],
                                      scores_all[:, sbase:sbase + 4096]))
                psS_free[sl] = (esem, ev)
                if last_ev is None:
                    last_ev = {}
                last_ev[esem] = ev
            if V in last_ev:
                wait(pl, "vector", V, last_ev[V])
            if SC in last_ev:
                need("vector", SC, last_ev[SC])
            vm = tick(pl, "vector", V,
                      lambda e, sb0=sb0, sbase=sbase, ncols=ncols:
                      e.reduce_max(st[:, sb0:sb0 + 1],
                                   scores_all[:, sbase:sbase + ncols],
                                   axis=AX))
            wait(pl, "vector", V, vm)
            vneg = tick(pl, "vector", V, lambda e, sb0=sb0:
                        e.tensor_scalar_mul(st[:, sb0 + 1:sb0 + 2],
                                            st[:, sb0:sb0 + 1], -scale))
            slot_info[i] = [ext, vneg, 0]

        def qk_stage2(i):
            ext, vneg, _ = slot_info[i]
            par = i % 4
            sb0 = par * 4
            sbase = par * 4096
            ncols = ext * 128
            need("scalar", V, vneg)
            need("scalar", *p_busy[par])
            exp_last = tick(pl, "scalar", SC,
                            lambda e, sbase=sbase, ncols=ncols, sb0=sb0:
                            e.activation(scores_all[:, sbase:sbase + ncols],
                                         scores_all[:, sbase:sbase + ncols],
                                         ACT.Exp,
                                         bias=st[:, sb0 + 1:sb0 + 2],
                                         scale=scale))
            sc_busy[par] = exp_last
            slot_info[i][2] = exp_last

        def trans_slot(i):
            ext, vneg, last_exp = slot_info.pop(i)
            par = i % 4
            sbase = par * 4096
            cb_ticks = []
            nT = len(psT2)
            need("tensor", SC, last_exp)
            for g in range(ceil_div(ext, 4)):
                nb = min(4, ext - g * 4)
                need("tensor", SC, psT_free[g % nT])
                def _d(e, g=g, nb=nb, sbase=sbase, sl=g % nT):
                    r = None
                    for j in range(nb):
                        kb = g * 4 + j
                        r = e.matmul(psT2[sl][:, j * 128:(j + 1) * 128],
                                     scores_all[:, sbase + kb * 128:
                                                sbase + (kb + 1) * 128],
                                     identH[:, :], start=True, stop=True)
                    return r
                tg = tick(pl, "tensor", T, _d)
                need("scalar", T, tg)
                sgc = tick(pl, "scalar", SC,
                           lambda e, g=g, nb=nb, sbase=sbase, sl=g % nT:
                           e.copy(scores_all[:, sbase + g * 512:
                                             sbase + g * 512 + nb * 128],
                                  psT2[sl][:, 0:nb * 128]))
                psT_free[g % nT] = sgc
                cb_ticks.append(sgc)
            return cb_ticks

        def dav_pair(i0):
            if av_head_hook is not None:
                av_head_hook(i0)
            exts = [ext_fn(i0), ext_fn(i0 + 1)]
            par0 = i0 % 4
            cb0 = trans_slot(i0)
            cb1 = trans_slot(i0 + 1)
            ext_hi = max(exts)
            sl4a, sl4b = par0, par0 + 1
            zt = 0
            if exts[0] < exts[1]:
                zt = tick(pl, "vector", V, lambda e, par0=par0, exts=exts:
                          e.memset(scores_all[:, par0 * 4096 + exts[0] * 128:
                                              par0 * 4096 + exts[1] * 128],
                                   0.0))
            need("tensor", *psL_free[sl4a])
            need("tensor", *psL_free[sl4b])
            need("tensor", SC, max(cb0[-1], cb1[-1]))
            need("tensor", V, zt)
            def _lm(e, par0=par0, ext_hi=ext_hi):
                r = None
                for kb in range(ext_hi):
                    r = e.matmul(psL[0:1, par0 * 128:(par0 + 2) * 128],
                                 onesH[:, 0:1], pslice(par0, kb, 2),
                                 start=(kb == 0), stop=(kb == ext_hi - 1))
                return r
            tl = tick(pl, "tensor", T, _lm)
            gbs = []
            for k in range(2):
                sl4 = par0 + k
                need("vector", T, tl)
                vcp = tick(pl, "vector", V, lambda e, sl4=sl4:
                           e.tensor_copy(lrow_sb[0:1,
                                                 sl4 * 128:(sl4 + 1) * 128],
                                         psL[0:1,
                                             sl4 * 128:(sl4 + 1) * 128]))
                need("tensor", V, vcp)
                tb_ = tick(pl, "tensor", T, lambda e, sl4=sl4:
                           e.matmul(psL[0:128, sl4 * 128:(sl4 + 1) * 128],
                                    onesR[0:1, :],
                                    lrow_sb[0:1, sl4 * 128:(sl4 + 1) * 128],
                                    start=True, stop=True))
                psL_free[sl4] = (T, tb_)
                need("vector", T, tb_)
                need("vector", *bc_busy[sl4])
                gb = tick(pl, "vector", V, lambda e, sl4=sl4:
                          e.reciprocal(linvB[:, sl4 * 128:(sl4 + 1) * 128],
                                       psL[0:128,
                                           sl4 * 128:(sl4 + 1) * 128]))
                gbs.append(gb)
            need("tensor", *psAV_free[sl4a])
            need("tensor", *psAV_free[sl4b])
            av_ap = av_out_slice(par0, 2)
            tp_now = av_tilepos() if callable(av_tilepos) else av_tilepos
            for kb in range(ext_hi):
                need("tensor", SC, cb0[min(kb // 4, len(cb0) - 1)])
                need("tensor", SC, cb1[min(kb // 4, len(cb1) - 1)])
                lhs_ap = av_lhs(kb)
                def _av(e, kb=kb, par0=par0, ext_hi=ext_hi, av_ap=av_ap,
                        lhs_ap=lhs_ap, tp_=tp_now):
                    return e.matmul(av_ap, lhs_ap, pslice(par0, kb, 2),
                                    start=(kb == 0), stop=(kb == ext_hi - 1),
                                    tile_position=tp_)
                if kb == ext_hi - 1:
                    tav = tick(pl, "tensor", T, _av)
                else:
                    emit(pl, "tensor", lambda e, f=_av: f(e))
            p_busy[sl4a] = (T, tav)
            p_busy[sl4b] = (T, tav)
            for k, i in enumerate((i0, i0 + 1)):
                sl4 = par0 + k
                ve = evac_fn(i, sl4, tav, gbs[k])
                bc_busy[sl4] = (V, ve)
                psAV_free[sl4] = (V, ve)

        if prep_fn is not None:
            prep_fn(0)
        for i in range(n_slots):
            if prep_fn is not None and i + 1 < n_slots:
                prep_fn(i + 1)
            if i >= 4 and i % 2 == 0:
                dav_pair(i - 4)
            qk_stage1(i)
            if i >= 1:
                qk_stage2(i - 1)
        qk_stage2(n_slots - 1)
        dav_pair(n_slots - 4)
        dav_pair(n_slots - 2)

    def ln_stream_ring(psT2, src_fn, ntiles, on_tile=None, *, ln=True,
                       src_f32=True, bounce=None, dst=None, dst_colmap=None):
        NT = ntiles * 128
        ring = None if dst is not None else sb(nm("ring"), [128, 2 * 1024], BF16)
        xin = sb(nm("lnx"), [128, 2048], F32 if src_f32 else BF16)
        xnb = sb(nm("lnn"), [128, 2048], BF16) if ln else xin
        st12 = sb(nm("lns"), [128, 12], F32)
        mv = sb(nm("lnm"), [128, 2], F32)
        sml = sb(nm("lnv"), [128, 4], F32)
        xin_free = [(V, 0), (V, 0)]
        xnb_free = [(dB[0], 0), (dB[1], 0)]
        ring_free = [(None, 0), (None, 0)]
        psT_free = [0, 0]
        for t in range(ntiles):
            sl = t % 2
            co = sl * 1024
            need("sync", *xin_free[sl])
            ld = dma(pl, "sync", dX[sl], xin[:, co:co + 1024], src_fn(t))
            if ln:
                def _bn0(e, a=st12, c=co):
                    e.bn_stats(a[:, 0:6], xin[:, c:c + 512])
                    return e.bn_stats(a[:, 6:12], xin[:, c + 512:c + 1024])
                need("vector", dX[sl], ld)
                v0 = tick(pl, "vector", V, _bn0)
                wait(pl, "vector", V, v0)
                v1 = tick(pl, "vector", V,
                          lambda e, a=st12, m=mv: e.bn_aggr(m[:, :], a[:, :]))
                need("scalar", V, v1)
                need("scalar", V, konst_rdy)
                tick(pl, "scalar", SC, lambda e: e.activation(
                    sml[:, 0:1], mv[:, 1:2], ACT.Sqrt, bias=konst[:, 1:2]))
                need("vector", SC, SC.n)
                vf0 = tick(pl, "vector", V, lambda e:
                           e.reciprocal(sml[:, 1:2], sml[:, 0:1]))
                wait(pl, "vector", V, vf0)
                vf1 = tick(pl, "vector", V, lambda e:
                           e.tensor_tensor(sml[:, 2:3], mv[:, 0:1],
                                           sml[:, 1:2], op=AluOpType.mult))
                wait(pl, "vector", V, vf1)
                v2 = tick(pl, "vector", V, lambda e:
                          e.tensor_scalar_mul(sml[:, 3:4], sml[:, 2:3], -1.0))
                wait(pl, "vector", V, v2)
                need("vector", *xnb_free[sl])
                s2 = tick(pl, "vector", V, lambda e, c=co: e.tensor_scalar(
                    xnb[:, c:c + 1024], xin[:, c:c + 1024],
                    sml[:, 1:2], sml[:, 3:4],
                    op0=AluOpType.mult, op1=AluOpType.add))
                xin_free[sl] = (V, s2)
                tsrc = (V, s2)
            else:
                tsrc = (dX[sl], ld)
            if bounce is not None:
                need("sync", tsrc[0], tsrc[1])
                xnb_free[sl] = (dB[sl], dma(pl, "sync", dB[sl], bounce(t),
                                            xnb[:, co:co + 1024]))
            nT_ = len(psT2)
            for g in range(2):
                gb_ = g % nT_
                need("tensor", *tsrc)
                need("tensor", V, psT_free[gb_])
                if ring is not None:
                    need("tensor", *ring_free[sl])
                def _tr(e, c=co, g=g, gb_=gb_, sl=sl):
                    r = None
                    for j in range(4):
                        ch = g * 4 + j
                        r = e.matmul(psT2[gb_][:, j * 128:(j + 1) * 128],
                                     xnb[:, c + ch * 128:c + (ch + 1) * 128],
                                     identB[:, :], start=True, stop=True)
                    return r
                tg = tick(pl, "tensor", T, _tr)
                need("vector", T, tg)
                if ring is not None:
                    cp = tick(pl, "vector", V,
                              lambda e, g=g, gb_=gb_, sl=sl:
                              e.tensor_copy(
                                  ring[:, sl * 1024 + g * 512:
                                       sl * 1024 + (g + 1) * 512],
                                  psT2[gb_][:, :]))
                else:
                    t2 = dst_colmap(t) if dst_colmap else t
                    def _cpd(e, g=g, gb_=gb_, t2=t2):
                        o = dst[:, :].rearrange("p (c q) -> p c q", q=NT)
                        return e.tensor_copy(
                            o[:, g * 4:g * 4 + 4, t2 * 128:(t2 + 1) * 128],
                            psT2[gb_][:, :].rearrange("p (c q) -> p c q",
                                                      q=128))
                    cp = tick(pl, "vector", V, _cpd)
                psT_free[gb_] = cp
            if not ln:
                xin_free[sl] = (T, T.n)
            rdy = (V, V.n)
            if on_tile is not None:
                ring_free[sl] = on_tile(
                    t, None if ring is None else
                    ring[:, sl * 1024:(sl + 1) * 1024], rdy)
        return None

    def phase_A(dst_res):
        pa = ExitStack()
        satt = ExitStack()
        sb_o, ps_o = CUR[0], CURP[0]
        CUR[0] = pa
        CURP[0] = satt
        psT2 = [ps(nm("psT"), [128, 512]) for _ in range(2)]
        psS = [ps(nm("psS"), [128, 1024]) for _ in range(2)]
        psAV = ps(nm("psAV"), [128, 512])
        psL = ps(nm("psL"), [128, 512])
        psM = psS
        psM_free = [0, 0]
        CURP[0] = pa

        maskA = sb(nm("maskA"), [128, NSLOT * 256], F32)
        for i in range(NSLOT):
            dma(pl, "gpsimd", dG, maskA[:, i * 256:(i + 1) * 256], maskc[i])
        wuk_sb = sb(nm("wuk"), [128, 2 * D], BF16)
        load_w(wuk_sb, wuk, DC, D)
        wuv_sb = sb(nm("wuv"), [128, 2 * D], BF16)
        load_w(wuv_sb, wuv, DC, D)
        wo_sb = sb(nm("wo"), [128, 8 * D], BF16)
        load_w(wo_sb, wo, D, D)
        c_kvT = sb(nm("ckvT"), [128, 2 * S], BF16)
        QT = sb(nm("QT"), [128, 8 * SQ], BF16)
        outTa = sb(nm("outTa"), [128, 8 * SQ], BF16)

        s1 = ExitStack()
        CUR[0] = s1
        wdq_sb = sb(nm("wdq"), [128, 8 * DC], BF16)
        load_w(wdq_sb, wdq, D, DC)
        wuq_sb = sb(nm("wuq"), [128, 2 * D], BF16)
        load_w(wuq_sb, wuq, DC, D)
        wload1 = dG.n
        c_qT = sb(nm("cqT"), [128, 2 * SQ], BF16)

        def own_tile(t, ring_ap, rdy):
            need("tensor", *rdy)
            need("tensor", dG, wload1)
            last = 0
            for rc in range(2):
                sl = rc
                need("tensor", V, psM_free[sl])
                def _mm(e, rc=rc, sl=sl, r=ring_ap):
                    q = None
                    for k in range(8):
                        q = e.matmul(psM[sl][0:128, 0:128],
                                     wdq_sb[:, k * DC + rc * 128:
                                            k * DC + (rc + 1) * 128],
                                     r[:, k * 128:(k + 1) * 128],
                                     start=(k == 0), stop=(k == 7))
                    return q
                tg = tick(pl, "tensor", T, _mm)
                last = tg
                need("vector", T, tg)
                psM_free[sl] = tick(pl, "vector", V,
                                    lambda e, rc=rc, t=t, sl=sl:
                                    e.tensor_copy(
                                        c_qT[:, rc * SQ + t * 128:
                                             rc * SQ + (t + 1) * 128],
                                        psM[sl][0:128, 0:128]))
            return (T, last)

        ln_stream_ring(psT2, lambda t: x_own[t * 128:(t + 1) * 128, :],
                       NSLOT, own_tile)
        for pr in range(8):
            for (noff, nw) in col_chunks(SQ):
                sl = (pr * 4 + noff // 512) % 2
                need("tensor", V, psM_free[sl])
                def _mm(e, pr=pr, noff=noff, nw=nw, sl=sl):
                    q = None
                    for rc in range(2):
                        q = e.matmul(psM[sl][0:128, 0:nw],
                                     wuq_sb[:, rc * D + pr * 128:
                                            rc * D + (pr + 1) * 128],
                                     c_qT[:, rc * SQ + noff:rc * SQ + noff + nw],
                                     start=(rc == 0), stop=(rc == 1))
                    return q
                tg = tick(pl, "tensor", T, _mm)
                need("vector", T, tg)
                psM_free[sl] = tick(pl, "vector", V,
                                    lambda e, pr=pr, noff=noff, nw=nw, sl=sl:
                                    e.tensor_copy(
                                        QT[:, pr * SQ + noff:pr * SQ + noff + nw],
                                        psM[sl][0:128, 0:nw]))
        CUR[0] = pa
        s1.close()
        bar()

        s2 = ExitStack()
        CUR[0] = s2
        wdkv_sb = sb(nm("wdkv"), [128, 8 * DC], BF16)
        load_w(wdkv_sb, wdkv, D, DC)
        wload2 = dG.n

        def full_tile(t, ring_ap, rdy):
            need("tensor", *rdy)
            need("tensor", dG, wload2)
            last = 0
            for rc in range(2):
                sl = rc
                need("tensor", V, psM_free[sl])
                def _mm(e, rc=rc, sl=sl, r=ring_ap):
                    q = None
                    for k in range(8):
                        q = e.matmul(psM[sl][0:128, 0:128],
                                     wdkv_sb[:, k * DC + rc * 128:
                                             k * DC + (rc + 1) * 128],
                                     r[:, k * 128:(k + 1) * 128],
                                     start=(k == 0), stop=(k == 7))
                    return q
                tg = tick(pl, "tensor", T, _mm)
                last = tg
                need("vector", T, tg)
                psM_free[sl] = tick(pl, "vector", V,
                                    lambda e, rc=rc, t=t, sl=sl:
                                    e.tensor_copy(
                                        c_kvT[:, rc * S + t * 128:
                                              rc * S + (t + 1) * 128],
                                        psM[sl][0:128, 0:128]))
            return (T, last)

        ln_stream_ring(psT2, lambda t: x_full[t * 128:(t + 1) * 128, :],
                       NKB, full_tile)
        CUR[0] = pa
        s2.close()
        bar()

        KTp = sb(nm("KTp"), [128, S], BF16)
        Vp = sb(nm("Vp"), [128, S], F16)
        scores_all = sb(nm("ssb"), [128, 4 * 4096], F16)
        st = sb(nm("stA"), [128, 16], F32)
        lrow_sb = sb(nm("lrow"), [1, 512], F32)
        linvB = sb(nm("linvB"), [128, 512], F32)
        onesH = sb(nm("ones1"), [128, 1], F16)
        onesR = sb(nm("onesr"), [1, 128], F32)

        def _vone(e):
            e.memset(onesH[:, :], 1.0)
            return e.memset(onesR[:, :], 1.0)
        vone = tick(pl, "vector", V, _vone)
        need("tensor", V, vone)

        need("vector", dG, dG.n)
        for hp in range(8):
            need("tensor", SC, SC.n)
            need("tensor", V, V.n)
            ktv_last = [0]
            for (noff, nw) in col_chunks(S):
                sl = (noff // 512) % 2
                need("tensor", V, psM_free[sl])
                def _mmk(e, noff=noff, nw=nw, sl=sl, hp=hp):
                    q = None
                    for rc in range(2):
                        q = e.matmul(psM[sl][0:128, 0:nw],
                                     wuk_sb[:, rc * D + hp * 128:
                                            rc * D + (hp + 1) * 128],
                                     c_kvT[:, rc * S + noff:rc * S + noff + nw],
                                     start=(rc == 0), stop=(rc == 1))
                    return q
                tg = tick(pl, "tensor", T, _mmk)
                need("vector", T, tg)
                psM_free[sl] = tick(pl, "vector", V,
                                    lambda e, noff=noff, nw=nw, sl=sl:
                                    e.tensor_copy(KTp[:, noff:noff + nw],
                                                  psM[sl][0:128, 0:nw]))
            for kb in range(NKB):
                sl = kb % 2
                need("tensor", V, psM_free[sl])
                def _mmv(e, kb=kb, sl=sl, hp=hp):
                    q = None
                    for rc in range(2):
                        q = e.matmul(psM[sl][0:128, 0:128],
                                     c_kvT[:, rc * S + kb * 128:
                                           rc * S + (kb + 1) * 128],
                                     wuv_sb[:, rc * D + hp * 128:
                                            rc * D + (hp + 1) * 128],
                                     start=(rc == 0), stop=(rc == 1))
                    return q
                tg = tick(pl, "tensor", T, _mmv)
                need("vector", T, tg)
                psM_free[sl] = tick(pl, "vector", V,
                                    lambda e, kb=kb, sl=sl:
                                    e.tensor_copy(Vp[:, kb * 128:(kb + 1) * 128],
                                                  psM[sl][0:128, 0:128]))
                ktv_last[0] = psM_free[sl]

            ktv_rdy = (V, V.n)
            need("tensor", *ktv_rdy)

            def vmap(vs):
                hh = (vs >> 1) & 1
                i = ((vs >> 2) << 1) | (vs & 1)
                return hh, i

            def qk_fn(vs, koff, scols, psum, sl, hp=hp):
                hh, i = vmap(vs)
                h0 = hh * 64
                def _q(e, i=i, koff=koff, scols=scols, psum=psum, h0=h0):
                    q = None
                    for (no, nw) in col_chunks(scols):
                        q = e.matmul(psum[0:128, no:no + nw],
                                     QT[h0:h0 + 64,
                                        hp * SQ + i * 128:
                                        hp * SQ + (i + 1) * 128],
                                     KTp[h0:h0 + 64,
                                         koff + no:koff + no + nw],
                                     start=True, stop=True)
                    return q
                tick(pl, "tensor", T, _q)

            def seg_evac(e, vs, sgi, koff, scols, psum, ssb):
                hh, i = vmap(vs)
                ext = 2 * i + 2
                moff = ext * 128 - 256 - koff
                if 0 <= moff < scols:
                    if moff > 0:
                        e.tensor_copy(ssb[:, koff:koff + moff],
                                      psum[0:128, 0:moff])
                    return e.tensor_tensor(
                        ssb[:, koff + moff:koff + moff + 256],
                        psum[0:128, moff:moff + 256],
                        maskA[:, i * 256:(i + 1) * 256],
                        op=AluOpType.add)
                return e.copy(ssb[:, koff:koff + scols],
                              psum[0:128, 0:scols])

            def seg_eng(vs, sgi):
                hh, i = vmap(vs)
                ext = 2 * i + 2
                nsg_ = ceil_div(ext * 128, 1024)
                return "vector" if sgi == nsg_ - 1 else "scalar"

            def av_lhs(kb):
                return Vp[:, kb * 128 + av_head[0] * 64:
                          kb * 128 + av_head[0] * 64 + 64]

            av_head = [0]

            def av_out(sl4, nsl):
                return psAV[av_head[0] * 64:av_head[0] * 64 + 64,
                            sl4 * 128:(sl4 + 1) * 128 + (nsl - 1) * 128]

            def evac(vs, sl4, tav, gb, hp=hp):
                hh, i = vmap(vs)
                h0 = hh * 64
                need("vector", T, tav)
                wait(pl, "vector", V, gb)
                return tick(pl, "vector", V,
                            lambda e, i=i, sl4=sl4, h0=h0:
                            e.tensor_tensor(
                                outTa[h0:h0 + 64,
                                      hp * SQ + i * 128:
                                      hp * SQ + (i + 1) * 128],
                                psAV[h0:h0 + 64,
                                     sl4 * 128:(sl4 + 1) * 128],
                                linvB[h0:h0 + 64,
                                      sl4 * 128:(sl4 + 1) * 128],
                                op=AluOpType.mult))

            def ext_fn(vs):
                hh, i = vmap(vs)
                return 2 * i + 2

            orig_av_lhs = av_lhs

            def av_lhs2(kb):
                return orig_av_lhs(kb)

            psS_free = [0, 0]
            attn_slots(psS, psS_free, psT2, psAV, psL, scores_all,
                       st, lrow_sb, linvB, onesH, onesR,
                       0.125, qk_fn, seg_evac, av_lhs2, av_out,
                       lambda: (0, av_head[0] * 64), evac, ext_fn,
                       seg_evac_eng=seg_eng, n_slots=2 * NSLOT,
                       av_head_hook=lambda i0: av_head.__setitem__(
                           0, vmap(i0)[0]))
        bar()
        satt.close()

        projst = sb(nm("projst"), [128, 8 * 512], BF16)
        xoldt = sb(nm("xold"), [128, 2 * 1024], F32)
        xnewt = sb(nm("xnew"), [128, 2 * 1024], F32)
        psW = [ps(nm("psW"), [128, 512]) for _ in range(2)]
        psR = ps(nm("psR"), [128, 1024])
        psW_free = [0, 0]
        xold_free = [0, 0]
        xnew_free = [0, 0]
        prev_tb = [0]
        prev_vn = [0]
        for q4 in range(4):
            pj_ticks = []
            for c in range(8):
                sl = c % 2
                need("tensor", SC, psW_free[sl])
                def _mw(e, q4=q4, c=c, sl=sl):
                    q = None
                    for c2 in range(8):
                        q = e.matmul(psW[sl][0:128, 0:512],
                                     wo_sb[:, c2 * D + c * 128:
                                           c2 * D + (c + 1) * 128],
                                     outTa[:, c2 * SQ + q4 * 512:
                                           c2 * SQ + (q4 + 1) * 512],
                                     start=(c2 == 0), stop=(c2 == 7))
                    return q
                tg = tick(pl, "tensor", T, _mw)
                need("scalar", T, tg)
                need("scalar", T, prev_tb[0])
                sgc = tick(pl, "scalar", SC, lambda e, c=c, sl=sl:
                           e.copy(projst[:, c * 512:(c + 1) * 512],
                                  psW[sl][0:128, 0:512]))
                psW_free[sl] = sgc
                pj_ticks.append(sgc)
            for tt4 in range(4):
                t = q4 * 4 + tt4
                sl = t % 2
                need("sync", V, xold_free[sl])
                ld = dma(pl, "sync", dO[sl], xoldt[:, sl * 1024:(sl + 1) * 1024],
                         x_own[t * 128:(t + 1) * 128, :])
                need("tensor", SC, pj_ticks[-1])
                need("tensor", V, prev_vn[0])
                def _tb(e, tt4=tt4):
                    q = None
                    for c in range(8):
                        q = e.matmul(psR[:, c * 128:(c + 1) * 128],
                                     projst[:, c * 512 + tt4 * 128:
                                            c * 512 + (tt4 + 1) * 128],
                                     identB[:, :], start=True, stop=True)
                    return q
                tg = tick(pl, "tensor", T, _tb)
                prev_tb[0] = tg
                need("vector", T, tg)
                need("vector", dO[sl], ld)
                need("vector", dN[sl], xnew_free[sl])
                vn = tick(pl, "vector", V, lambda e, sl=sl:
                          e.tensor_tensor(xnewt[:, sl * 1024:(sl + 1) * 1024],
                                          xoldt[:, sl * 1024:(sl + 1) * 1024],
                                          psR[:, :], op=AluOpType.add))
                xold_free[sl] = vn
                prev_vn[0] = vn
                need("sync", V, vn)
                xnew_free[sl] = dma(pl, "sync", dN[sl],
                                    dst_res[t * 128:(t + 1) * 128, :],
                                    xnewt[:, sl * 1024:(sl + 1) * 1024])
        bar()
        CUR[0] = sb_o
        CURP[0] = ps_o
        pa.close()

    def phase_ffn(src_res, dst_res):
        pf = ExitStack()
        sb_o, ps_o = CUR[0], CURP[0]
        CUR[0] = pf
        CURP[0] = pf
        psT2 = [ps(nm("psT"), [128, 512]) for _ in range(2)]
        psH = [ps(nm("psH"), [128, 512]) for _ in range(3)]
        psO = [ps(nm("psO"), [128, 512]) for _ in range(3)]
        psH_free = [(SC, 0), (SC, 0), (SC, 0)]
        psO_free = [0, 0, 0]

        xT = sb(nm("xT"), [128, 8 * SQ], BF16)
        b1c = sb(nm("b1c"), [128, 128], F32)
        b1n = sb(nm("b1n"), [8, HID], BF16)
        b2sb = sb(nm("b2sb"), [8, D], BF16)
        wredsb = sb(nm("wred"), [128, 8 * GD], BF16)
        wgf = sb(nm("wgf"), [8, 64], F32)
        wgnb = sb(nm("wgnb"), [8, GD], BF16)
        wgT = sb(nm("wgT"), [16, 8], BF16)
        xredT = sb(nm("xredT"), [16, SQ], BF16)
        lgT = sb(nm("lgT"), [8, SQ], BF16)
        ge8 = sb(nm("ge8"), [128, 8], F32)
        gsc = sb(nm("gsc"), [128, 4], F32)
        ohs = sb(nm("ohs"), [128, NSLOT * 8], F32)
        ohb = sb(nm("ohb"), [128, 8], BF16)
        ohsT = sb(nm("ohsT"), [8, SQ], BF16)

        dma(pl, "gpsimd", dG, b1n[0:8, :], b1m)
        dma(pl, "gpsimd", dG, b2sb[0:8, :], b2m)
        for k in range(8):
            dma(pl, "gpsimd", dG, wredsb[:, k * GD:(k + 1) * GD],
                wred[k * 128:(k + 1) * 128, :])
        ldwg = dma(pl, "sync", dW, wgf[0:8, 0:GD], wgm)
        cload = dG.n

        _lns = ExitStack()
        CUR[0] = _lns
        ln_stream_ring(psT2, lambda t: src_res[t * 128:(t + 1) * 128, :],
                       NSLOT, dst=xT)
        CUR[0] = pf
        bar()
        _lns.close()
        o_acc = sb(nm("oacc"), [128, 16 * 1024], F32)
        _exps = ExitStack()
        CUR[0] = _exps
        hT = sb(nm("hT"), [128, 4 * SQ], BF16)
        w1sb = [sb(nm("w1sb"), [128, 8 * HID], BF16) for _ in range(2)]
        w2sb = sb(nm("w2sb"), [128, 4 * 1024], BF16)
        CUR[0] = pf

        need("tensor", dG, cload)
        b1_ticks = []
        for c in range(16):
            sl = c % 2
            need("tensor", *psH_free[sl])
            tg = tick(pl, "tensor", T, lambda e, c=c, sl=sl:
                      e.matmul(psH[sl][0:128, 0:8],
                               b1n[0:8, c * 128:(c + 1) * 128],
                               identB[0:8, 0:8], start=True, stop=True))
            need("scalar", T, tg)
            sgc = tick(pl, "scalar", SC, lambda e, c=c, sl=sl:
                       e.copy(b1c[:, c * 8:(c + 1) * 8],
                              psH[sl][0:128, 0:8]))
            psH_free[sl] = (SC, sgc)
            b1_ticks.append(sgc)

        need("vector", dW, ldwg)
        vw0 = tick(pl, "vector", V, lambda e:
                   e.tensor_tensor(wgf[0:8, 16:32], wgf[0:8, 0:16],
                                   wgf[0:8, 0:16], op=AluOpType.mult))
        wait(pl, "vector", V, vw0)
        v1 = tick(pl, "vector", V, lambda e:
                  e.reduce_sum(wgf[0:8, 32:33], wgf[0:8, 16:32], axis=AX))
        need("scalar", V, v1)
        need("scalar", V, konst_rdy)
        s1 = tick(pl, "scalar", SC, lambda e:
                  e.activation(wgf[0:8, 33:34], wgf[0:8, 32:33], ACT.Sqrt,
                               bias=konst[0:8, 0:1]))
        need("vector", SC, s1)
        vwa = tick(pl, "vector", V, lambda e:
                   e.tensor_scalar_max(wgf[0:8, 34:35], wgf[0:8, 33:34],
                                       1e-4))
        wait(pl, "vector", V, vwa)
        vwb = tick(pl, "vector", V, lambda e:
                   e.reciprocal(wgf[0:8, 35:36], wgf[0:8, 34:35]))
        wait(pl, "vector", V, vwb)
        vwc = tick(pl, "vector", V, lambda e:
                   e.tensor_scalar(wgf[0:8, 36:52], wgf[0:8, 0:16],
                                   wgf[0:8, 35:36], None,
                                   op0=AluOpType.mult))
        wait(pl, "vector", V, vwc)
        v2 = tick(pl, "vector", V, lambda e:
                  e.tensor_copy(wgnb[0:8, :], wgf[0:8, 36:52]))
        need("tensor", V, v2)
        need("tensor", *psH_free[0])
        tg = tick(pl, "tensor", T, lambda e:
                  e.matmul(psH[0][0:16, 0:8], wgnb[0:8, :],
                           identB[0:8, 0:8], start=True, stop=True))
        need("vector", T, tg)
        vwgT = tick(pl, "vector", V, lambda e:
                    e.tensor_copy(wgT[0:16, :], psH[0][0:16, 0:8]))
        psH_free[0] = (V, vwgT)
        for (noff, nw) in col_chunks(SQ):
            sl = (noff // 512) % 2
            need("tensor", *psH_free[sl])
            def _mx(e, noff=noff, nw=nw, sl=sl):
                q = None
                for k in range(8):
                    q = e.matmul(psH[sl][0:16, 0:nw],
                                 wredsb[:, k * GD:(k + 1) * GD],
                                 xT[:, k * SQ + noff:k * SQ + noff + nw],
                                 start=(k == 0), stop=(k == 7))
                return q
            tg = tick(pl, "tensor", T, _mx)
            need("vector", T, tg)
            psH_free[sl] = (V, tick(pl, "vector", V,
                                lambda e, noff=noff, nw=nw, sl=sl:
                                e.tensor_copy(xredT[0:16, noff:noff + nw],
                                              psH[sl][0:16, 0:nw])))
        for (noff, nw) in col_chunks(SQ):
            sl = (noff // 512) % 2
            need("tensor", V, psO_free[sl])
            def _ml(e, noff=noff, nw=nw, sl=sl):
                return e.matmul(psO[sl][0:8, 0:nw], wgT[0:16, :],
                                xredT[0:16, noff:noff + nw],
                                start=True, stop=True)
            tg = tick(pl, "tensor", T, _ml)
            need("vector", T, tg)
            psO_free[sl] = tick(pl, "vector", V,
                                lambda e, noff=noff, nw=nw, sl=sl:
                                e.tensor_copy(lgT[0:8, noff:noff + nw],
                                              psO[sl][0:8, 0:nw]))
        prev_tg2 = [0]
        for t in range(NSLOT):
            sl = t % 2
            need("tensor", *psH_free[sl])
            tg = tick(pl, "tensor", T, lambda e, t=t, sl=sl:
                      e.matmul(psH[sl][0:128, 0:8],
                               lgT[0:8, t * 128:(t + 1) * 128],
                               identB[0:8, 0:8], start=True, stop=True))
            need("vector", T, tg)
            vg0 = tick(pl, "vector", V, lambda e, sl=sl:
                       e.reduce_max(gsc[:, 0:1], psH[sl][0:128, 0:8],
                                    axis=AX))
            wait(pl, "vector", V, vg0)
            v1 = tick(pl, "vector", V, lambda e:
                      e.tensor_scalar_mul(gsc[:, 1:2], gsc[:, 0:1], -1.0))
            need("scalar", V, v1)
            s1 = tick(pl, "scalar", SC, lambda e, sl=sl:
                      e.activation(ge8[:, :], psH[sl][0:128, 0:8], ACT.Exp,
                                   bias=gsc[:, 1:2]))
            need("vector", SC, s1)
            need("vector", T, prev_tg2[0])
            vgl = tick(pl, "vector", V, lambda e:
                       e.reduce_sum(gsc[:, 2:3], ge8[:, :], axis=AX))
            wait(pl, "vector", V, vgl)
            v2a__ = tick(pl, "vector", V, lambda e, t=t, sl=sl:
                         e.reciprocal(gsc[:, 3:4], gsc[:, 2:3]))
            wait(pl, "vector", V, v2a__)
            v2b__ = tick(pl, "vector", V, lambda e, t=t, sl=sl:
                         e.tensor_scalar(ohs[:, t * 8:(t + 1) * 8],
                                         psH[sl][0:128, 0:8], gsc[:, 0:1],
                                         None, op0=AluOpType.is_equal))
            wait(pl, "vector", V, v2b__)
            v2c__ = tick(pl, "vector", V, lambda e, t=t:
                         e.tensor_scalar(ohs[:, t * 8:(t + 1) * 8],
                                         ohs[:, t * 8:(t + 1) * 8],
                                         gsc[:, 3:4], None,
                                         op0=AluOpType.mult))
            wait(pl, "vector", V, v2c__)
            v2 = tick(pl, "vector", V, lambda e, t=t:
                      e.tensor_copy(ohb[:, :], ohs[:, t * 8:(t + 1) * 8]))
            psH_free[sl] = (V, v2)
            need("tensor", V, v2)
            need("tensor", V, psO_free[sl])
            tg2 = tick(pl, "tensor", T, lambda e, sl=sl:
                       e.matmul(psO[sl][0:8, 0:128], ohb[:, :],
                                identB[:, :], start=True, stop=True))
            prev_tg2[0] = tg2
            need("vector", T, tg2)
            vT = tick(pl, "vector", V, lambda e, t=t, sl=sl:
                      e.tensor_copy(ohsT[0:8, t * 128:(t + 1) * 128],
                                    psO[sl][0:8, 0:128]))
            psO_free[sl] = vT
        for t in range(NSLOT):
            for do in range(2):
                sl = (t * 2 + do) % 2
                need("tensor", *psH_free[sl])
                tg = tick(pl, "tensor", T, lambda e, t=t, do=do, sl=sl:
                          e.matmul(psH[sl][0:128, 0:512],
                                   ohsT[0:8, t * 128:(t + 1) * 128],
                                   b2sb[0:8, do * 512:(do + 1) * 512],
                                   start=True, stop=True))
                need("vector", T, tg)
                psH_free[sl] = (V, tick(
                    pl, "vector", V, lambda e, t=t, do=do, sl=sl:
                    e.tensor_copy(
                        o_acc[:, t * 1024 + do * 512:
                              t * 1024 + (do + 1) * 512],
                        psH[sl][0:128, 0:512])))

        w1_free = [(T, 0), (T, 0)]
        w1_rdy = {}
        w2_free = [0]
        prev_oT = [0]

        def load_w1(e_i):
            wb = e_i % 2
            need("gpsimd", *w1_free[wb])
            for d_ in range(8):
                tgt = dma(pl, "gpsimd", dW1[wb],
                          w1sb[wb][:, d_ * HID:(d_ + 1) * HID],
                          W1[e_i, d_ * 128:(d_ + 1) * 128, :])
            w1_rdy[e_i] = (dW1[wb], tgt)

        load_w1(0)
        for e_i in range(E):
            if e_i + 1 < E:
                load_w1(e_i + 1)
            wb = e_i % 2
            last_mm = 0
            for qr in range(4):
                need("gpsimd", T, w2_free[0])
                for k in range(4):
                    w2ld = dma(pl, "gpsimd", dW2, w2sb[:, k * 1024:(k + 1) * 1024],
                               W2[e_i, qr * 512 + k * 128:
                                  qr * 512 + (k + 1) * 128, :])
                h_ticks = []
                for hc in range(4):
                    cg = qr * 4 + hc
                    for t4 in range(4):
                        psl = (hc * 4 + t4) % 3
                        need("tensor", *w1_rdy[e_i])
                        need("tensor", *psH_free[psl])
                        def _mh(e, wb=wb, qr=qr, hc=hc, t4=t4, psl=psl):
                            q = None
                            for k in range(8):
                                q = e.matmul(psH[psl][0:128, 0:512],
                                             w1sb[wb][:, k * HID +
                                                      qr * 512 + hc * 128:
                                                      k * HID +
                                                      qr * 512 +
                                                      (hc + 1) * 128],
                                             xT[:, k * SQ + t4 * 512:
                                                k * SQ + (t4 + 1) * 512],
                                             start=(k == 0), stop=(k == 7))
                            return q
                        tg = tick(pl, "tensor", T, _mh)
                        last_mm = tg
                        need("scalar", T, tg)
                        need("scalar", T, prev_oT[0])
                        sgc = tick(pl, "scalar", SC,
                                   lambda e, hc=hc, t4=t4, psl=psl, cg=cg,
                                   e_i=e_i:
                                   e.activation(
                                       hT[:, hc * SQ + t4 * 512:
                                          hc * SQ + (t4 + 1) * 512],
                                       psH[psl][0:128, 0:512], ACT.Gelu,
                                       bias=b1c[:, cg * 8 + e_i:
                                                cg * 8 + e_i + 1]))
                        psH_free[psl] = (SC, sgc)
                        h_ticks.append(sgc)
                need("tensor", dW2, w2ld)
                need("tensor", SC, h_ticks[-1])
                for tt in range(NSLOT):
                    for do in range(2):
                        sl = (tt * 2 + do) % 3
                        need("tensor", V, psO_free[sl])
                        def _mo(e, tt=tt, do=do, sl=sl):
                            q = None
                            for k in range(4):
                                q = e.matmul(psO[sl][0:128, 0:512],
                                             hT[:, k * SQ + tt * 128:
                                                k * SQ + (tt + 1) * 128],
                                             w2sb[:, k * 1024 + do * 512:
                                                  k * 1024 + (do + 1) * 512],
                                             start=(k == 0), stop=(k == 3))
                            return q
                        tg = tick(pl, "tensor", T, _mo)
                        prev_oT[0] = tg
                        need("vector", T, tg)
                        psO_free[sl] = tick(
                            pl, "vector", V,
                            lambda e, tt=tt, do=do, sl=sl, e_i=e_i:
                            e.scalar_tensor_tensor(
                                o_acc[:, tt * 1024 + do * 512:
                                      tt * 1024 + (do + 1) * 512],
                                psO[sl][0:128, 0:512],
                                ohs[:, tt * 8 + e_i:tt * 8 + e_i + 1],
                                o_acc[:, tt * 1024 + do * 512:
                                      tt * 1024 + (do + 1) * 512],
                                op0=AluOpType.mult, op1=AluOpType.add))
                w2_free[0] = prev_oT[0]
            w1_free[wb] = (T, last_mm)

        bar()
        _exps.close()
        xoldt = sb(nm("xold"), [128, 2 * 1024], F32)
        xnewt = sb(nm("xnew"), [128, 2 * 1024], F32)
        xold_free = [0, 0]
        xnew_free = [0, 0]
        for t in range(NSLOT):
            sl = t % 2
            need("sync", V, xold_free[sl])
            ld = dma(pl, "sync", dO[sl], xoldt[:, sl * 1024:(sl + 1) * 1024],
                     src_res[t * 128:(t + 1) * 128, :])
            need("vector", dO[sl], ld)
            need("vector", dN[sl], xnew_free[sl])
            vn = tick(pl, "vector", V, lambda e, t=t, sl=sl:
                      e.tensor_tensor(xnewt[:, sl * 1024:(sl + 1) * 1024],
                                      xoldt[:, sl * 1024:(sl + 1) * 1024],
                                      o_acc[:, t * 1024:(t + 1) * 1024],
                                      op=AluOpType.add))
            xold_free[sl] = vn
            need("sync", V, vn)
            xnew_free[sl] = dma(pl, "sync", dN[sl],
                                dst_res[t * 128:(t + 1) * 128, :],
                                xnewt[:, sl * 1024:(sl + 1) * 1024])
        bar()
        CUR[0] = sb_o
        CURP[0] = ps_o
        pf.close()

    def phase_C(src_res, dst_res):
        pc = ExitStack()
        satt = ExitStack()
        sb_o, ps_o = CUR[0], CURP[0]
        CUR[0] = pc
        CURP[0] = satt
        psT2 = [ps(nm("psT"), [128, 512])]
        psS = [ps(nm("psS"), [128, 1024]) for _ in range(2)]
        psAV = ps(nm("psAV"), [128, 512])
        psRaw = ps(nm("psRaw"), [128, 512])
        psL = ps(nm("psL"), [128, 512])
        psM = psS
        psM_free = [0, 0]
        CURP[0] = pc

        xnT3 = sb(nm("xnT3"), [128, 8 * S], BF16)
        QTb = sb(nm("QTb"), [128, 8 * SQ], BF16)
        rebc = sb(nm("rebc"), [128, 8], F32)
        rpbc = sb(nm("rpbc"), [128, 8], F32)
        rbn = sb(nm("rbn"), [8, 2 * DB], BF16)
        dma(pl, "gpsimd", dH, rbn[0:8, 0:DB], rebm)
        rbload = dma(pl, "gpsimd", dH, rbn[0:8, DB:2 * DB], rpbm)

        need("tensor", dH, rbload)
        tg = tick(pl, "tensor", T, lambda e:
                  e.matmul(psRaw[0:128, 0:8], rbn[0:8, 0:DB],
                           identB[0:8, 0:8], start=True, stop=True))
        need("vector", T, tg)
        v_ = tick(pl, "vector", V, lambda e:
                  e.tensor_copy(rebc[:, :], psRaw[0:128, 0:8]))
        need("tensor", V, v_)
        tg = tick(pl, "tensor", T, lambda e:
                  e.matmul(psRaw[0:128, 0:8], rbn[0:8, DB:2 * DB],
                           identB[0:8, 0:8], start=True, stop=True))
        need("vector", T, tg)
        tick(pl, "vector", V, lambda e:
             e.tensor_copy(rpbc[:, :], psRaw[0:128, 0:8]))

        s1 = ExitStack()
        CUR[0] = s1
        wqs_sb = sb(nm("wqs"), [128, 8 * D], BF16)
        wqload = load_w(wqs_sb, wqs, D, D)

        def qtb_tile(t, ring_ap, rdy):
            need("tensor", *rdy)
            need("tensor", dG, wqload)
            last = 0
            for pr in range(8):
                sl = pr % 2
                need("tensor", V, psM_free[sl])
                def _mm(e, pr=pr, sl=sl, r=ring_ap):
                    q = None
                    for k in range(8):
                        q = e.matmul(psM[sl][0:128, 0:128],
                                     wqs_sb[:, k * D + pr * 128:
                                            k * D + (pr + 1) * 128],
                                     r[:, k * 128:(k + 1) * 128],
                                     start=(k == 0), stop=(k == 7))
                    return q
                tg = tick(pl, "tensor", T, _mm)
                last = tg
                need("vector", T, tg)
                psM_free[sl] = tick(pl, "vector", V,
                                    lambda e, pr=pr, t=t, sl=sl:
                                    e.tensor_copy(
                                        QTb[:, pr * SQ + t * 128:
                                            pr * SQ + (t + 1) * 128],
                                        psM[sl][0:128, 0:128]))
            return (T, last)

        ln_stream_ring(psT2, lambda t: src_res[t * 128:(t + 1) * 128, :],
                       NSLOT, qtb_tile,
                       bounce=lambda t: ccin.ap()[t * 128:(t + 1) * 128, :])
        CUR[0] = pc
        s1.close()
        bar()

        need("gpsimd", dW, dW.n)
        def _cc(e):
            return e.collective_compute(
                "AllGather", mybir.AluOpType.bypass,
                replica_groups=(cc_groups if cc_groups is not None else
                                [[0, 1], [2, 3], [4, 5], [6, 7]]),
                ins=[ccin.ap().opt()], outs=[ccout.ap().opt()])
        tick(pl, "gpsimd", CCs, _cc)
        ccdone = CCs.n
        need("sync", CCs, ccdone)

        ln_stream_ring(psT2, lambda t: ccout.ap()[t * 128:(t + 1) * 128, :],
                       NKB, dst=xnT3, ln=False, src_f32=False,
                       dst_colmap=_gath_global_block)
        bar()

        if DEBUG:
            dma(pl, "gpsimd", dG, dbg_xres, src_res)
            dma(pl, "gpsimd", dG, dbg_ccout, ccout.ap())
        shd = ExitStack()
        CUR[0] = shd
        KTh = sb(nm("KTh"), [128, S], BF16)
        Vh = sb(nm("Vh"), [128, S], F16)
        relkT = sb(nm("relkT"), [128, POSW - 128], BF16)
        qreb = sb(nm("qreb"), [128, 2 * 128], BF16)
        qrpb = sb(nm("qrpb"), [128, 2 * 128], BF16)
        scores_all = sb(nm("ssb"), [128, 4 * 4096], F16)
        stC = sb(nm("stC"), [128, 16], F32)
        lrow_sb = sb(nm("lrow"), [1, 512], F32)
        linvB = sb(nm("linvB"), [128, 512], F32)
        onesH = sb(nm("ones1"), [128, 1], F16)
        onesR = sb(nm("onesr"), [1, 128], F32)

        def _vone(e):
            e.memset(onesH[:, :], 1.0)
            return e.memset(onesR[:, :], 1.0)
        vone = tick(pl, "vector", V, _vone)
        need("tensor", V, vone)
        diag2 = [sb(nm("diag"), [128, 4096], F16) for _ in range(2)]
        rawf = sb(nm("rawf"), [128, RWIN], F16)
        outTbst = sb(nm("outTbst"), [128, 2 * 512], BF16)
        wks_h = sb(nm("wksh"), [128, 8 * 128], BF16)
        wvs_h = sb(nm("wvsh"), [128, 8 * 128], BF16)
        wrel_h = sb(nm("wrelh"), [128, 2 * 128], BF16)
        posb_a = sb(nm("posba"), [128, 512], BF16)
        posb_b = sb(nm("posbb"), [128, 512], BF16)
        zpad = tick(pl, "vector", V, lambda e:
                    e.memset(posb_b[:, :], 0.0))
        wrelz = [0]

        diag_rdy = {}
        raw_free = [(dRW, 0)]
        raw_sc = [0]
        last_diag_read = [0, 0]
        diag_free = [(V, 0), (V, 0)]
        spill_tgt = [0]
        spill_grp = [0, 0]

        for h in range(HB):
            need("tensor", SC, SC.n)
            need("tensor", V, V.n)
            need("vector", dG, spill_tgt[0])
            need("gpsimd", T, T.n)
            for k in range(8):
                dma(pl, "gpsimd", dH, wks_h[:, k * 128:(k + 1) * 128],
                    wks[k * 128:(k + 1) * 128, h * 128:(h + 1) * 128])
                dma(pl, "gpsimd", dH, wvs_h[:, k * 128:(k + 1) * 128],
                    wvs[k * 128:(k + 1) * 128, h * 128:(h + 1) * 128])
            if h == 0:
                wrelz[0] = tick(pl, "vector", V, lambda e:
                                e.memset(wrel_h[:, 128:256], 0.0))
            need("gpsimd", V, wrelz[0])
            dma(pl, "gpsimd", dH, wrel_h[0:128, 0:128],
                wrel[0:128, h * 128:(h + 1) * 128])
            whload = dma(pl, "gpsimd", dH, wrel_h[0:64, 128:256],
                         wrel[128:192, h * 128:(h + 1) * 128])
            scl = 1.0 / math.sqrt(DB)
            vq = V.n
            qrpb_state = {"free": (T, 0)}
            qreb_state = {0: (T, 0), 1: (T, 0)}
            for (noff, nw) in col_chunks(S):
                sl = (noff // 512) % 2
                need("tensor", V, psM_free[sl])
                need("tensor", dH, whload)
                def _mk(e, noff=noff, nw=nw, sl=sl):
                    q = None
                    for k in range(8):
                        q = e.matmul(psM[sl][0:128, 0:nw],
                                     wks_h[:, k * 128:(k + 1) * 128],
                                     xnT3[:, k * S + noff:k * S + noff + nw],
                                     start=(k == 0), stop=(k == 7))
                    return q
                tg = tick(pl, "tensor", T, _mk)
                need("vector", T, tg)
                psM_free[sl] = tick(pl, "vector", V,
                                    lambda e, noff=noff, nw=nw, sl=sl:
                                    e.tensor_copy(KTh[:, noff:noff + nw],
                                                  psM[sl][0:128, 0:nw]))
            for kb in range(NKB):
                sl = kb % 2
                need("tensor", V, psM_free[sl])
                def _mv(e, kb=kb, sl=sl):
                    q = None
                    for k in range(8):
                        q = e.matmul(psM[sl][0:128, 0:128],
                                     xnT3[:, k * S + kb * 128:
                                          k * S + (kb + 1) * 128],
                                     wvs_h[:, k * 128:(k + 1) * 128],
                                     start=(k == 0), stop=(k == 7))
                    return q
                tg = tick(pl, "tensor", T, _mv)
                need("vector", T, tg)
                psM_free[sl] = tick(pl, "vector", V,
                                    lambda e, kb=kb, sl=sl:
                                    e.tensor_copy(Vh[:, kb * 128:(kb + 1) * 128],
                                                  psM[sl][0:128, 0:128]))
            pos_free = [(T, 0), (T, 0)]
            for (noff, nw) in col_chunks(POSW - 128):
                sl = (noff // 512) % 2
                need("gpsimd", T, pos_free[0][1])
                need("gpsimd", V, zpad)
                dma(pl, "gpsimd", dP[0], posb_a[:, 0:nw],
                    posc[0:128, 128 + noff:128 + noff + nw])
                pld = dma(pl, "gpsimd", dP[0],
                          posb_b[0:64, 0:nw],
                          posc[128:192, 128 + noff:128 + noff + nw])
                need("tensor", dP[0], pld)
                need("tensor", V, psM_free[sl])
                def _mr(e, noff=noff, nw=nw, sl=sl):
                    e.matmul(psM[sl][0:128, 0:nw], wrel_h[0:128, 0:128],
                             posb_a[:, 0:nw],
                             start=True, stop=False)
                    return e.matmul(psM[sl][0:128, 0:nw],
                                    wrel_h[0:128, 128:256],
                                    posb_b[0:128, 0:nw],
                                    start=False, stop=True)
                tg = tick(pl, "tensor", T, _mr)
                pos_free[0] = (T, tg)
                need("vector", T, tg)
                psM_free[sl] = tick(pl, "vector", V,
                                    lambda e, noff=noff, nw=nw, sl=sl:
                                    e.tensor_copy(relkT[:, noff:noff + nw],
                                                  psM[sl][0:128, 0:nw]))

            def prep_fn(i, h=h):
                par = i % 2
                qsl = i % 2
                wstart = 3968 - 256 * i
                need("vector", *qrpb_state["free"])
                vqp = tick(pl, "vector", V, lambda e, i=i, h=h, qsl=qsl:
                           e.tensor_scalar(qrpb[:, qsl * 128:(qsl + 1) * 128],
                                           QTb[:, h * SQ + i * 128:
                                               h * SQ + (i + 1) * 128],
                                           scl, rpbc[:, h:h + 1],
                                           op0=AluOpType.mult,
                                           op1=AluOpType.add))
                need("tensor", V, vqp)
                need("scalar", *raw_free[0])
                last_rm = 0
                for (noff, nw) in col_chunks(RWIN):
                    need("tensor", SC, raw_sc[0])
                    def _rm(e, i=i, noff=noff, nw=nw, ws=wstart - 128,
                            qsl=qsl):
                        return e.matmul(psRaw[0:128, 0:nw],
                                        qrpb[:, qsl * 128:(qsl + 1) * 128],
                                        relkT[:, ws + noff:ws + noff + nw],
                                        start=True, stop=True)
                    tg = tick(pl, "tensor", T, _rm)
                    last_rm = tg
                    need("scalar", T, tg)
                    raw_sc[0] = tick(pl, "scalar", SC,
                                     lambda e, noff=noff, nw=nw:
                                     e.copy(rawf[:, noff:noff + nw],
                                            psRaw[0:128, 0:nw]))
                qrpb_state["free"] = (T, last_rm)
                last_sc = raw_sc[0]
                need("sync", SC, last_sc)
                need("sync", dD[par], last_diag_read[par])
                base = relscr[par]
                w2d = base[0:128 * RPITCH].rearrange("(p j) -> p j", j=RPITCH)
                wt = dma(pl, "sync", dRW, w2d[:, 0:RWIN], rawf[:, :])
                raw_free[0] = (dRW, wt)
                need("scalar", dRW, wt)
                need("scalar", *diag_free[i % 2])
                dv = base[127:127 + 128 * RWIN].rearrange("(p k) -> p k",
                                                          k=RWIN)
                dt = dma(pl, "scalar", dD[par], diag2[i % 2][:, :],
                         dv[:, 0:4096])
                last_diag_read[par] = dt
                diag_rdy[i] = (dD[par], dt)

            def qk_fn(i, koff, scols, psum, sl, h=h):
                qsl = i % 2
                if koff == 0:
                    need("vector", *qreb_state[qsl])
                    vqb = tick(pl, "vector", V, lambda e, i=i, h=h, qsl=qsl:
                               e.tensor_scalar(
                                   qreb[:, qsl * 128:(qsl + 1) * 128],
                                   QTb[:, h * SQ + i * 128:
                                       h * SQ + (i + 1) * 128],
                                   scl, rebc[:, h:h + 1],
                                   op0=AluOpType.mult, op1=AluOpType.add))
                    need("tensor", V, vqb)
                def _q(e, i=i, koff=koff, scols=scols, psum=psum, qsl=qsl):
                    q = None
                    for (no, nw) in col_chunks(scols):
                        q = e.matmul(psum[0:128, no:no + nw],
                                     qreb[:, qsl * 128:(qsl + 1) * 128],
                                     KTh[:, koff + no:koff + no + nw],
                                     start=True, stop=True)
                    return q
                tick(pl, "tensor", T, _q)
                qreb_state[qsl] = (T, T.n)

            def seg_evac(e, i, sgi, koff, scols, psum, ssb):
                return e.tensor_tensor(ssb[:, koff:koff + scols],
                                       psum[0:128, 0:scols],
                                       diag2[i % 2][:, koff:koff + scols],
                                       op=AluOpType.add)

            def av_lhs(kb):
                return Vh[:, kb * 128:(kb + 1) * 128]

            def av_out(sl4, nsl):
                return psAV[0:128, sl4 * 128:(sl4 + nsl) * 128]

            def evac(i, sl4, tav, gb, h=h):
                need("vector", T, tav)
                wait(pl, "vector", V, gb)
                g2 = (i // 4) % 2
                need("vector", dSp[g2], spill_grp[g2])
                vt = tick(pl, "vector", V,
                          lambda e, i=i, sl4=sl4, g2=g2:
                          e.tensor_tensor(
                              outTbst[:, g2 * 512 + (i % 4) * 128:
                                      g2 * 512 + (i % 4 + 1) * 128],
                              psAV[0:128, sl4 * 128:(sl4 + 1) * 128],
                              linvB[:, sl4 * 128:(sl4 + 1) * 128],
                              op=AluOpType.mult))
                diag_free[i % 2] = (V, vt)
                if i % 4 == 3:
                    need("gpsimd", V, vt)
                    spill_grp[g2] = dma(
                        pl, "gpsimd", dSp[g2],
                        outbsp[h][:, (i - 3) * 128:(i + 1) * 128],
                        outTbst[:, g2 * 512:(g2 + 1) * 512])
                return vt

            if DEBUG and h == 0:
                need("gpsimd", V, V.n)
                dma(pl, "gpsimd", dG, dbg_hd[0][:, 0:SQ], qreb[:, :])
                dma(pl, "gpsimd", dG, dbg_hd[0][:, SQ:2 * SQ],
                    QTb[:, 0:SQ])
                dma(pl, "gpsimd", dG, dbg_hd[1], KTh[:, :])
                dma(pl, "gpsimd", dG, dbg_hd[2], Vh[:, :])
                dma(pl, "gpsimd", dG, dbg_hd[3][:, 0:8], rebc[:, :])
                dma(pl, "gpsimd", dG, dbg_hd[3][:, 8:16], rpbc[:, :])
                dma(pl, "gpsimd", dG, dbg_hd[3][0:8, 16:16 + 2 * DB],
                    rbn[0:8, :])
            psS_free = [0, 0]
            attn_slots(psS, psS_free, psT2, psAV, psL, scores_all,
                       stC, lrow_sb, linvB, onesH, onesR,
                       1.0, qk_fn, seg_evac, av_lhs, av_out,
                       None, evac, lambda i: NKB,
                       diag_rdy_fn=lambda i: diag_rdy[i],
                       prep_fn=prep_fn)
            spill_tgt[0] = spill_grp[0]
        if DEBUG:
            for hh_ in range(HB):
                dma(pl, "gpsimd", dG, dbg_outb[hh_], outbsp[hh_])
        bar()
        CUR[0] = pc
        shd.close()
        satt.close()

        outTb = sb(nm("outTb"), [128, 8 * SQ], BF16)
        wout_sb = sb(nm("wout"), [128, 8 * D], BF16)
        boutc = sb(nm("boutc"), [128, 8], F32)
        bon = sb(nm("bon"), [8, 128], BF16)
        projst = sb(nm("projst"), [128, 8 * 512], BF16)
        xoldt = sb(nm("xold"), [128, 2 * 1024], F32)
        xnewt = sb(nm("xnew"), [128, 2 * 1024], F32)
        psW = [ps(nm("psW"), [128, 512]) for _ in range(2)]
        psR = ps(nm("psR"), [128, 1024])
        load_w(wout_sb, wout, D, D)
        for h in range(HB):
            dma(pl, "gpsimd", dG, outTb[:, h * SQ:(h + 1) * SQ], outbsp[h])
        bload = dma(pl, "gpsimd", dG, bon[0:8, :],
                    boutv.rearrange("(c p) -> c p", p=128))
        need("tensor", dG, bload)
        tg = tick(pl, "tensor", T, lambda e:
                  e.matmul(psR[0:128, 0:8], bon[0:8, :],
                           identB[0:8, 0:8], start=True, stop=True))
        need("vector", T, tg)
        vb = tick(pl, "vector", V, lambda e:
                  e.tensor_copy(boutc[:, :], psR[0:128, 0:8]))
        psW_free = [0, 0]
        xold_free = [0, 0]
        xnew_free = [0, 0]
        prev_tb = [0]
        prev_vn = [0]
        need("tensor", V, vb)
        for q4 in range(4):
            pj_ticks = []
            for c in range(8):
                sl = c % 2
                need("tensor", V, psW_free[sl])
                def _mw(e, q4=q4, c=c, sl=sl):
                    q = None
                    for c2 in range(8):
                        q = e.matmul(psW[sl][0:128, 0:512],
                                     wout_sb[:, c2 * D + c * 128:
                                             c2 * D + (c + 1) * 128],
                                     outTb[:, c2 * SQ + q4 * 512:
                                           c2 * SQ + (q4 + 1) * 512],
                                     start=(c2 == 0), stop=(c2 == 7))
                    return q
                tg = tick(pl, "tensor", T, _mw)
                need("vector", T, tg)
                need("vector", T, prev_tb[0])
                sgc = tick(pl, "vector", V, lambda e, c=c, sl=sl:
                           e.tensor_scalar(projst[:, c * 512:(c + 1) * 512],
                                           psW[sl][0:128, 0:512],
                                           boutc[:, c:c + 1], None,
                                           op0=AluOpType.add))
                psW_free[sl] = sgc
                pj_ticks.append(sgc)
            for tt4 in range(4):
                t = q4 * 4 + tt4
                sl = t % 2
                need("sync", V, xold_free[sl])
                ld = dma(pl, "sync", dO[sl], xoldt[:, sl * 1024:(sl + 1) * 1024],
                         src_res[t * 128:(t + 1) * 128, :])
                need("tensor", V, pj_ticks[-1])
                need("tensor", V, prev_vn[0])
                def _tb(e, tt4=tt4):
                    q = None
                    for c in range(8):
                        q = e.matmul(psR[:, c * 128:(c + 1) * 128],
                                     projst[:, c * 512 + tt4 * 128:
                                            c * 512 + (tt4 + 1) * 128],
                                     identB[:, :], start=True, stop=True)
                    return q
                tg = tick(pl, "tensor", T, _tb)
                prev_tb[0] = tg
                need("vector", T, tg)
                need("vector", dO[sl], ld)
                need("vector", dN[sl], xnew_free[sl])
                vn = tick(pl, "vector", V, lambda e, sl=sl:
                          e.tensor_tensor(xnewt[:, sl * 1024:(sl + 1) * 1024],
                                          xoldt[:, sl * 1024:(sl + 1) * 1024],
                                          psR[:, :], op=AluOpType.add))
                xold_free[sl] = vn
                prev_vn[0] = vn
                need("sync", V, vn)
                xnew_free[sl] = dma(pl, "sync", dN[sl],
                                    dst_res[t * 128:(t + 1) * 128, :],
                                    xnewt[:, sl * 1024:(sl + 1) * 1024])
        bar()
        CUR[0] = sb_o
        CURP[0] = ps_o
        pc.close()

    if n_sub >= 1:
        phase_A(yout if n_sub == 1 else xres)
    if n_sub >= 2:
        phase_ffn(xres, yout if n_sub == 2 else xres)
    if n_sub >= 3:
        phase_C(xres, yout if n_sub == 3 else xres)
    if n_sub >= 4:
        phase_ffn(xres, yout)

    wait(pl, "sync", dW, dW.n)
    wait(pl, "gpsimd", dG, dG.n)
    wait(pl, "scalar", dS, dS.n)

    with ExitStack() as sem_stack:
        for s_ in ALLSEMS:
            s_.h = sem_stack.enter_context(nc.semaphore(s_.name))
        with nc.Block() as blk:
            blk.sync(lambda e: [f(e) for f in pl.ev["sync"]])
            blk.scalar(lambda e: [f(e) for f in pl.ev["scalar"]])
            blk.vector(lambda e: [f(e) for f in pl.ev["vector"]])
            blk.tensor(lambda e: [f(e) for f in pl.ev["tensor"]])
            blk.gpsimd(lambda e: [f(e) for f in pl.ev["gpsimd"]])
    es.close()
    return nc


def kernel(**inputs):
    f32 = lambda k: np.ascontiguousarray(np.asarray(inputs[k], np.float32))
    xin = f32("L0_attn_a_x")
    common = {
        "wdq": f32("L0_attn_a_W_dq"), "wuq": f32("L0_attn_a_W_uq"),
        "wdkv": f32("L0_attn_a_W_dkv"), "wuk": f32("L0_attn_a_W_uk"),
        "wuv": f32("L0_attn_a_W_uv"), "wo": f32("L0_attn_a_W_o"),
        "wqs": f32("L0_attn_b_w_qs"), "wks": f32("L0_attn_b_w_ks"),
        "wvs": f32("L0_attn_b_w_vs"), "wout": f32("L0_attn_b_w_out"),
        "wrel": f32("L0_attn_b_w_rel_pos_k"),
        "rebm": f32("L0_attn_b_rel_embed_bias").reshape(HB, DB),
        "rpbm": f32("L0_attn_b_rel_pos_bias").reshape(HB, DB),
        "boutv": f32("L0_attn_b_b_out"),
        "wred": f32("L0_ffn_w_red"), "wgm": f32("L0_ffn_wg"),
        "W1": f32("L0_ffn_W1"), "b1m": f32("L0_ffn_b1"),
        "W2": f32("L0_ffn_W2"), "b2m": f32("L0_ffn_b2"),
        "identc": np.eye(128, dtype=np.float32),
    }
    pos = [_pos_staged(0), _pos_staged(1)]
    msk = [_masks_staged(0), _masks_staged(1)]
    in_maps = []
    for c in range(8):
        b, par = c // 2, c % 2
        xb = xin[b]
        m = dict(common)
        m["x_own"] = np.ascontiguousarray(
            xb.reshape(NKB, 128, D)[par::2].reshape(SQ, D))
        m["x_full"] = xb
        m["posc"] = pos[par]
        m["maskc"] = msk[par]
        in_maps.append(m)
    nc = build_program()
    res = run_bass_kernel_spmd(nc, in_maps, core_ids=list(range(8)))
    out = np.zeros((B, S, D), np.float32)
    for c in range(8):
        b, par = c // 2, c % 2
        yb = np.asarray(res.results[c]["yout"]).reshape(NSLOT, 128, D)
        out[b].reshape(NKB, 128, D)[par::2] = yb
    return out


# version 14 (final)
# speedup vs baseline: 1.2689x; 1.1156x over previous
import math
import numpy as np
from contextlib import ExitStack

import concourse.bass as bass
import concourse.mybir as mybir
from concourse.alu_op_type import AluOpType
from concourse.bass_utils import run_bass_kernel_spmd
from concourse.library_config import attn as _gp_attn_lib

F32 = mybir.dt.float32
BF16 = mybir.dt.bfloat16
F16 = mybir.dt.float16
ACT = mybir.ActivationFunctionType
AX = mybir.AxisListType.X

B, S, D = 4, 4096, 1024
SQ = 2048
NSLOT = 16
NKB = 32
DC = 256
HA, DA = 16, 64
HB, DB = 8, 128
E, HID, GD = 8, 2048, 16
RWIN = 4224
RPITCH = RWIN + 1
POSW = 8192
N_SUB = 4
DEBUG = False
EPS = 1e-5
MASK_NEG = -60000.0

ENGINES = ("sync", "scalar", "vector", "tensor", "gpsimd")


def _positions_np():
    seq_len, feature_size = S, 192
    n = feature_size // 6
    distances = np.arange(-seq_len + 1, seq_len, dtype=np.float64)
    absd = np.abs(distances)[:, None]
    max_range = math.log(seq_len) / math.log(2.0)
    half_life = 2.0 ** np.linspace(3.0, max_range, n)
    feat_exp = np.exp(-math.log(2.0) / half_life[None, :] * absd)
    center_widths = 2.0 ** np.arange(1, n + 1, dtype=np.float64) - 1.0
    feat_cm = (center_widths[None, :] > absd).astype(np.float64)
    stddev = seq_len / (2.0 * n)
    start_mean = seq_len / n
    mean = np.linspace(start_mean, float(seq_len), n)[None, :]
    conc = (mean / stddev) ** 2
    rate = mean / stddev ** 2
    with np.errstate(divide="ignore", invalid="ignore"):
        log_unnorm = np.where(conc - 1.0 == 0.0, 0.0,
                              (conc - 1.0) * np.log(absd))
        log_unnorm = np.where(absd == 0.0,
                              np.where(conc - 1.0 > 0, -np.inf, 0.0),
                              log_unnorm) - rate * absd
    import math as _m
    _lg = np.frompyfunc(_m.lgamma, 1, 1)
    log_norm = _lg(conc).astype(np.float64) - conc * np.log(rate)
    probs = np.exp(log_unnorm - log_norm) + 1e-08
    feat_gamma = probs / probs.max(axis=-1, keepdims=True)
    emb = np.concatenate([feat_exp, feat_cm, feat_gamma], axis=-1)
    emb = np.concatenate([emb, np.sign(distances)[:, None] * emb], axis=-1)
    return emb.astype(np.float32)


def _pos_staged(par):
    pos = _positions_np().T
    out = np.zeros((192, POSW), np.float32)
    if par == 0:
        out[:, :8191] = pos
    else:
        out[:, 128:128 + 8063] = pos[:, :8063]
    return out


def _masks_staged(par):
    m = np.zeros((NSLOT, 128, 256), np.float32)
    qq = np.arange(128)
    kk = np.arange(128)
    for i in range(NSLOT):
        if par == 0:
            m[i, :, 0:128] = np.where(kk[None, :] > qq[:, None], MASK_NEG, 0.0)
            m[i, :, 128:256] = MASK_NEG
        else:
            m[i, :, 128:256] = np.where(kk[None, :] > qq[:, None], MASK_NEG, 0.0)
    return m


def _gath_global_block(c):
    return 2 * c if c < 16 else 2 * (c - 16) + 1


class Plan:
    def __init__(self):
        self.ev = {e: [] for e in ENGINES}

    def add(self, e, fn):
        self.ev[e].append(fn)


class Sem:
    __slots__ = ("name", "h", "n")

    def __init__(self, name):
        self.name = name
        self.h = None
        self.n = 0


def dma(pl, q, sem, out_ap, in_ap):
    def f(e, o=out_ap, i=in_ap, s=sem):
        e.dma_start(out=o, in_=i).then_inc(s.h, 16)
    pl.add(q, f)
    sem.n += 16
    return sem.n


def wait(pl, e, sem, val):
    pl.add(e, lambda eng, s=sem, v=val: eng.wait_ge(s.h, v))


def tick(pl, e, sem, fn):
    def g(eng, f=fn, s=sem):
        f(eng).then_inc(s.h, 1)
    pl.add(e, g)
    sem.n += 1
    return sem.n


def emit(pl, e, fn):
    pl.add(e, fn)


def barrier(pl, compute_sems, owner, queues):
    for e in ENGINES:
        for s_ in compute_sems:
            if owner.get(s_) != e:
                wait(pl, e, s_, s_.n)
        for qe, qsems in queues.items():
            for qsem in qsems:
                wait(pl, e, qsem, qsem.n)


def ceil_div(a, b):
    return (a + b - 1) // b


def col_chunks(total, step=512):
    out = []
    off = 0
    while off < total:
        w = min(step, total - off)
        out.append((off, w))
        off += w
    return out


RACECHECK = False

def build_program(n_sub=N_SUB, cc_groups=None):
    nc = bass.Bass("TRN2", target_bir_lowering=False,
                   detect_race_conditions=RACECHECK)
    pl = Plan()

    def din(name, shape, dt=F32):
        return nc.dram_tensor(name, shape, dt, kind="ExternalInput").ap()

    x_own = din("x_own", [SQ, D])
    x_full = din("x_full", [S, D])
    wdq = din("wdq", [D, DC]);   wuq = din("wuq", [DC, D])
    wdkv = din("wdkv", [D, DC]); wuk = din("wuk", [DC, D]); wuv = din("wuv", [DC, D])
    wo = din("wo", [D, D])
    wqs = din("wqs", [D, D]); wks = din("wks", [D, D]); wvs = din("wvs", [D, D])
    wout = din("wout", [D, D]); wrel = din("wrel", [192, D])
    rebm = din("rebm", [HB, DB]); rpbm = din("rpbm", [HB, DB])
    boutv = din("boutv", [D])
    wred = din("wred", [D, GD]); wgm = din("wgm", [E, GD])
    W1 = din("W1", [E, D, HID]); b1m = din("b1m", [E, HID])
    W2 = din("W2", [E, HID, D]); b2m = din("b2m", [E, D])
    identc = din("identc", [128, 128])
    maskc = din("maskc", [NSLOT, 128, 256])
    posc = din("posc", [192, POSW])

    yout = nc.dram_tensor("yout", [SQ, D], F32, kind="ExternalOutput").ap()
    if DEBUG:
        dbg_xres = nc.dram_tensor("dbg_xres", [SQ, D], F32,
                                  kind="ExternalOutput").ap()
        dbg_ccout = nc.dram_tensor("dbg_ccout", [2 * SQ, D], BF16,
                                   kind="ExternalOutput").ap()
        dbg_outb = nc.dram_tensor("dbg_outb", [HB, 128, SQ], BF16,
                                  kind="ExternalOutput").ap()
        dbg_p = nc.dram_tensor("dbg_p", [2, 128, 4096], BF16,
                               kind="ExternalOutput").ap()
        dbg_st = nc.dram_tensor("dbg_st", [2, 128, 64], F32,
                                kind="ExternalOutput").ap()
        dbg_hd = nc.dram_tensor("dbg_hd", [4, 128, S], BF16,
                                kind="ExternalOutput").ap()

    xres = nc.dram_tensor("xres", [SQ, D], F32).ap()
    relscr = nc.dram_tensor("relscr", [2, 128 * RPITCH + 512], F16).ap()
    outbsp = nc.dram_tensor("outbsp", [HB, 128, SQ], BF16).ap()
    ccin = nc.dram_tensor("ccin", [SQ, D], BF16)
    ccout = nc.dram_tensor("ccout", [2 * SQ, D], BF16)

    dW, dS, dG = Sem("dW"), Sem("dS"), Sem("dG")
    dX = [Sem("dX0"), Sem("dX1")]
    dB = [Sem("dB0"), Sem("dB1")]
    dO = [Sem("dO0"), Sem("dO1")]
    dN = [Sem("dN0"), Sem("dN1")]
    dRW = Sem("dRW")
    dD = [Sem("dD0"), Sem("dD1")]
    dW1 = [Sem("dW1a"), Sem("dW1b")]
    dW2 = Sem("dW2")
    dP = [Sem("dPa"), Sem("dPb")]
    dH = Sem("dH")
    dSp = [Sem("dSpa"), Sem("dSpb")]
    T, V, SC, GP, CCs, BARs = (Sem("T"), Sem("V"), Sem("SC"), Sem("GP"),
                               Sem("CC"), Sem("BAR"))
    ALLSEMS = [dW, dS, dG, dX[0], dX[1], dB[0], dB[1], dO[0], dO[1],
               dN[0], dN[1], dRW, dD[0], dD[1], dW1[0], dW1[1], dW2,
               dP[0], dP[1], dH, dSp[0], dSp[1], T, V, SC, GP, CCs, BARs]
    OWNER = {T: "tensor", V: "vector", SC: "scalar", GP: "gpsimd"}
    WM = {e: {} for e in ENGINES}
    queues = {"sync": [dW, dX[0], dX[1], dB[0], dB[1], dO[0], dO[1],
                       dN[0], dN[1], dRW],
              "scalar": [dS, dD[0], dD[1]],
              "gpsimd": [dG, dW1[0], dW1[1], dW2, dP[0], dP[1], dH,
                         dSp[0], dSp[1]]}

    def need(e, sem, val):
        if sem is None or val == "any" or val is None:
            return
        if isinstance(val, tuple):
            sem, val = val
        if val <= 0:
            return
        if OWNER.get(sem) == e:
            return
        if WM[e].get(sem, 0) >= val:
            return
        WM[e][sem] = val
        wait(pl, e, sem, val)

    def bar():
        barrier(pl, [T, V, SC, GP, CCs], OWNER, queues)
        for e in ENGINES:
            for s in ALLSEMS:
                if s is not BARs:
                    WM[e][s] = s.n

    es = ExitStack()
    CUR = [es]
    CURP = [es]

    def sb(name, shape, dt):
        return CUR[0].enter_context(nc.sbuf_tensor(name, shape, dt))

    def ps(name, shape):
        return CURP[0].enter_context(nc.psum_tensor(name, shape, F32))

    identB = sb("identB", [128, 128], BF16)
    dma(pl, "gpsimd", dG, identB[:, :], identc)
    identH = sb("identH", [128, 128], F16)
    ident_tgt = dma(pl, "gpsimd", dG, identH[:, :], identc)
    need("gpsimd", dG, ident_tgt)
    need("tensor", dG, ident_tgt)
    need("vector", dG, ident_tgt)
    konst = sb("konst", [128, 2], F32)

    def _k(e):
        e.memset(konst[:, 0:1], 0.0)
        return e.memset(konst[:, 1:2], EPS)
    konst_rdy = tick(pl, "vector", V, _k)

    uid = [0]

    def nm(p):
        uid[0] += 1
        return f"{p}{uid[0]}"

    def load_w(dst, src, R, C):
        for r in range(R // 128):
            dma(pl, "gpsimd", dG, dst[:, r * C:(r + 1) * C],
                src[r * 128:(r + 1) * 128, :])
        return dG.n

    def attn_slots(psS, psS_free, psT2, psAV, psL, scores_all, st,
                   lrow_sb, linvB, onesH, onesR, scale,
                   qk_fn, seg_evac_fn, av_lhs, av_out_slice, av_tilepos,
                   evac_fn, ext_fn, diag_rdy_fn=None, prep_fn=None,
                   seg_evac_eng=None, n_slots=NSLOT, av_head_hook=None):
        need("tensor", SC, SC.n)
        need("tensor", V, V.n)
        need("scalar", T, T.n)
        need("vector", T, T.n)
        psT_free = [0] * len(psT2)
        psAV_free = [(V, 0)] * 4
        psL_free = [(T, 0)] * 4
        bc_busy = [(V, 0)] * 4
        sc_busy = {k: 0 for k in range(4)}
        p_busy = {k: (T, 0) for k in range(4)}
        slot_info = {}

        def pslice(par, kb, nsl=1):
            v3 = scores_all[:, :].rearrange("p (s c) -> p s c", c=4096)
            return v3[:, par:par + nsl, kb * 128:(kb + 1) * 128]

        def qk_stage1(i):
            ext = ext_fn(i)
            ncols = ext * 128
            nsg = ceil_div(ncols, 1024)
            par = i % 4
            sb0 = par * 4
            sbase = par * 4096
            if diag_rdy_fn is not None:
                need("vector", *diag_rdy_fn(i))
                need("scalar", *diag_rdy_fn(i))
            need("vector", SC, sc_busy[par])
            need("vector", *p_busy[par])
            last_ev = None
            for sgi in range(nsg):
                koff = sgi * 1024
                scols = min(1024, ncols - koff)
                sl = sgi % 2
                if isinstance(psS_free[sl], tuple):
                    need("tensor", *psS_free[sl])
                qk_fn(i, koff, scols, psS[sl], sl)
                ts_ = T.n
                eng = seg_evac_eng(i, sgi) if seg_evac_eng else "vector"
                esem = V if eng == "vector" else SC
                need(eng, T, ts_)
                if eng == "scalar":
                    need(eng, SC, sc_busy[par])
                    need(eng, *p_busy[par])
                ev = tick(pl, eng, esem,
                          lambda e, i=i, sgi=sgi, koff=koff, scols=scols,
                          sl=sl, sbase=sbase:
                          seg_evac_fn(e, i, sgi, koff, scols, psS[sl],
                                      scores_all[:, sbase:sbase + 4096]))
                psS_free[sl] = (esem, ev)
                if last_ev is None:
                    last_ev = {}
                last_ev[esem] = ev
            if V in last_ev:
                wait(pl, "vector", V, last_ev[V])
            if SC in last_ev:
                need("vector", SC, last_ev[SC])
            vm = tick(pl, "vector", V,
                      lambda e, sb0=sb0, sbase=sbase, ncols=ncols:
                      e.reduce_max(st[:, sb0:sb0 + 1],
                                   scores_all[:, sbase:sbase + ncols],
                                   axis=AX))
            wait(pl, "vector", V, vm)
            vneg = tick(pl, "vector", V, lambda e, sb0=sb0:
                        e.tensor_scalar_mul(st[:, sb0 + 1:sb0 + 2],
                                            st[:, sb0:sb0 + 1], -scale))
            slot_info[i] = [ext, vneg, 0]

        def qk_stage2(i):
            ext, vneg, _ = slot_info[i]
            par = i % 4
            sb0 = par * 4
            sbase = par * 4096
            ncols = ext * 128
            need("scalar", V, vneg)
            need("scalar", *p_busy[par])
            exp_last = tick(pl, "scalar", SC,
                            lambda e, sbase=sbase, ncols=ncols, sb0=sb0:
                            e.activation(scores_all[:, sbase:sbase + ncols],
                                         scores_all[:, sbase:sbase + ncols],
                                         ACT.Exp,
                                         bias=st[:, sb0 + 1:sb0 + 2],
                                         scale=scale))
            sc_busy[par] = exp_last
            slot_info[i][2] = exp_last

        def trans_slot(i):
            ext, vneg, last_exp = slot_info.pop(i)
            par = i % 4
            sbase = par * 4096
            cb_ticks = []
            nT = len(psT2)
            need("tensor", SC, last_exp)
            for g in range(ceil_div(ext, 4)):
                nb = min(4, ext - g * 4)
                need("tensor", SC, psT_free[g % nT])
                def _d(e, g=g, nb=nb, sbase=sbase, sl=g % nT):
                    r = None
                    for j in range(nb):
                        kb = g * 4 + j
                        r = e.matmul(psT2[sl][:, j * 128:(j + 1) * 128],
                                     scores_all[:, sbase + kb * 128:
                                                sbase + (kb + 1) * 128],
                                     identH[:, :], start=True, stop=True)
                    return r
                tg = tick(pl, "tensor", T, _d)
                need("scalar", T, tg)
                sgc = tick(pl, "scalar", SC,
                           lambda e, g=g, nb=nb, sbase=sbase, sl=g % nT:
                           e.copy(scores_all[:, sbase + g * 512:
                                             sbase + g * 512 + nb * 128],
                                  psT2[sl][:, 0:nb * 128]))
                psT_free[g % nT] = sgc
                cb_ticks.append(sgc)
            return cb_ticks

        def dav_pair(i0):
            if av_head_hook is not None:
                av_head_hook(i0)
            exts = [ext_fn(i0), ext_fn(i0 + 1)]
            par0 = i0 % 4
            cb0 = trans_slot(i0)
            cb1 = trans_slot(i0 + 1)
            ext_hi = max(exts)
            sl4a, sl4b = par0, par0 + 1
            zt = 0
            if exts[0] < exts[1]:
                zt = tick(pl, "vector", V, lambda e, par0=par0, exts=exts:
                          e.memset(scores_all[:, par0 * 4096 + exts[0] * 128:
                                              par0 * 4096 + exts[1] * 128],
                                   0.0))
            need("tensor", *psL_free[sl4a])
            need("tensor", *psL_free[sl4b])
            need("tensor", SC, max(cb0[-1], cb1[-1]))
            need("tensor", V, zt)
            def _lm(e, par0=par0, ext_hi=ext_hi):
                r = None
                for kb in range(ext_hi):
                    r = e.matmul(psL[0:1, par0 * 128:(par0 + 2) * 128],
                                 onesH[:, 0:1], pslice(par0, kb, 2),
                                 start=(kb == 0), stop=(kb == ext_hi - 1))
                return r
            tl = tick(pl, "tensor", T, _lm)
            gbs = []
            for k in range(2):
                sl4 = par0 + k
                need("vector", T, tl)
                vcp = tick(pl, "vector", V, lambda e, sl4=sl4:
                           e.tensor_copy(lrow_sb[0:1,
                                                 sl4 * 128:(sl4 + 1) * 128],
                                         psL[0:1,
                                             sl4 * 128:(sl4 + 1) * 128]))
                need("tensor", V, vcp)
                tb_ = tick(pl, "tensor", T, lambda e, sl4=sl4:
                           e.matmul(psL[0:128, sl4 * 128:(sl4 + 1) * 128],
                                    onesR[0:1, :],
                                    lrow_sb[0:1, sl4 * 128:(sl4 + 1) * 128],
                                    start=True, stop=True))
                psL_free[sl4] = (T, tb_)
                need("vector", T, tb_)
                need("vector", *bc_busy[sl4])
                gb = tick(pl, "vector", V, lambda e, sl4=sl4:
                          e.reciprocal(linvB[:, sl4 * 128:(sl4 + 1) * 128],
                                       psL[0:128,
                                           sl4 * 128:(sl4 + 1) * 128]))
                gbs.append(gb)
            need("tensor", *psAV_free[sl4a])
            need("tensor", *psAV_free[sl4b])
            av_ap = av_out_slice(par0, 2)
            tp_now = av_tilepos() if callable(av_tilepos) else av_tilepos
            for kb in range(ext_hi):
                need("tensor", SC, cb0[min(kb // 4, len(cb0) - 1)])
                need("tensor", SC, cb1[min(kb // 4, len(cb1) - 1)])
                lhs_ap = av_lhs(kb)
                def _av(e, kb=kb, par0=par0, ext_hi=ext_hi, av_ap=av_ap,
                        lhs_ap=lhs_ap, tp_=tp_now):
                    return e.matmul(av_ap, lhs_ap, pslice(par0, kb, 2),
                                    start=(kb == 0), stop=(kb == ext_hi - 1),
                                    tile_position=tp_)
                if kb == ext_hi - 1:
                    tav = tick(pl, "tensor", T, _av)
                else:
                    emit(pl, "tensor", lambda e, f=_av: f(e))
            p_busy[sl4a] = (T, tav)
            p_busy[sl4b] = (T, tav)
            for k, i in enumerate((i0, i0 + 1)):
                sl4 = par0 + k
                ve = evac_fn(i, sl4, tav, gbs[k])
                bc_busy[sl4] = (V, ve)
                psAV_free[sl4] = (V, ve)

        if prep_fn is not None:
            prep_fn(0)
        for i in range(n_slots):
            if prep_fn is not None and i + 1 < n_slots:
                prep_fn(i + 1)
            if i >= 4 and i % 2 == 0:
                dav_pair(i - 4)
            qk_stage1(i)
            if i >= 1:
                qk_stage2(i - 1)
        qk_stage2(n_slots - 1)
        dav_pair(n_slots - 4)
        dav_pair(n_slots - 2)

    def ln_stream_ring(psT2, src_fn, ntiles, on_tile=None, *, ln=True,
                       src_f32=True, bounce=None, dst=None, dst_colmap=None):
        NT = ntiles * 128
        ring = None if dst is not None else sb(nm("ring"), [128, 2 * 1024], BF16)
        xin = sb(nm("lnx"), [128, 2048], F32 if src_f32 else BF16)
        xnb = sb(nm("lnn"), [128, 2048], BF16) if ln else xin
        st12 = sb(nm("lns"), [128, 12], F32)
        mv = sb(nm("lnm"), [128, 2], F32)
        sml = sb(nm("lnv"), [128, 4], F32)
        xin_free = [(V, 0), (V, 0)]
        xnb_free = [(dB[0], 0), (dB[1], 0)]
        ring_free = [(None, 0), (None, 0)]
        psT_free = [0, 0]
        for t in range(ntiles):
            sl = t % 2
            co = sl * 1024
            need("sync", *xin_free[sl])
            ld = dma(pl, "sync", dX[sl], xin[:, co:co + 1024], src_fn(t))
            if ln:
                def _bn0(e, a=st12, c=co):
                    e.bn_stats(a[:, 0:6], xin[:, c:c + 512])
                    return e.bn_stats(a[:, 6:12], xin[:, c + 512:c + 1024])
                need("vector", dX[sl], ld)
                v0 = tick(pl, "vector", V, _bn0)
                wait(pl, "vector", V, v0)
                v1 = tick(pl, "vector", V,
                          lambda e, a=st12, m=mv: e.bn_aggr(m[:, :], a[:, :]))
                need("scalar", V, v1)
                need("scalar", V, konst_rdy)
                tick(pl, "scalar", SC, lambda e: e.activation(
                    sml[:, 0:1], mv[:, 1:2], ACT.Sqrt, bias=konst[:, 1:2]))
                need("vector", SC, SC.n)
                vf0 = tick(pl, "vector", V, lambda e:
                           e.reciprocal(sml[:, 1:2], sml[:, 0:1]))
                wait(pl, "vector", V, vf0)
                vf1 = tick(pl, "vector", V, lambda e:
                           e.tensor_tensor(sml[:, 2:3], mv[:, 0:1],
                                           sml[:, 1:2], op=AluOpType.mult))
                wait(pl, "vector", V, vf1)
                v2 = tick(pl, "vector", V, lambda e:
                          e.tensor_scalar_mul(sml[:, 3:4], sml[:, 2:3], -1.0))
                wait(pl, "vector", V, v2)
                need("vector", *xnb_free[sl])
                s2 = tick(pl, "vector", V, lambda e, c=co: e.tensor_scalar(
                    xnb[:, c:c + 1024], xin[:, c:c + 1024],
                    sml[:, 1:2], sml[:, 3:4],
                    op0=AluOpType.mult, op1=AluOpType.add))
                xin_free[sl] = (V, s2)
                tsrc = (V, s2)
            else:
                tsrc = (dX[sl], ld)
            if bounce is not None:
                need("sync", tsrc[0], tsrc[1])
                xnb_free[sl] = (dB[sl], dma(pl, "sync", dB[sl], bounce(t),
                                            xnb[:, co:co + 1024]))
            nT_ = len(psT2)
            for g in range(2):
                gb_ = g % nT_
                need("tensor", *tsrc)
                need("tensor", V, psT_free[gb_])
                if ring is not None:
                    need("tensor", *ring_free[sl])
                def _tr(e, c=co, g=g, gb_=gb_, sl=sl):
                    r = None
                    for j in range(4):
                        ch = g * 4 + j
                        r = e.matmul(psT2[gb_][:, j * 128:(j + 1) * 128],
                                     xnb[:, c + ch * 128:c + (ch + 1) * 128],
                                     identB[:, :], start=True, stop=True)
                    return r
                tg = tick(pl, "tensor", T, _tr)
                need("vector", T, tg)
                if ring is not None:
                    cp = tick(pl, "vector", V,
                              lambda e, g=g, gb_=gb_, sl=sl:
                              e.tensor_copy(
                                  ring[:, sl * 1024 + g * 512:
                                       sl * 1024 + (g + 1) * 512],
                                  psT2[gb_][:, :]))
                else:
                    t2 = dst_colmap(t) if dst_colmap else t
                    def _cpd(e, g=g, gb_=gb_, t2=t2):
                        o = dst[:, :].rearrange("p (c q) -> p c q", q=NT)
                        return e.tensor_copy(
                            o[:, g * 4:g * 4 + 4, t2 * 128:(t2 + 1) * 128],
                            psT2[gb_][:, :].rearrange("p (c q) -> p c q",
                                                      q=128))
                    cp = tick(pl, "vector", V, _cpd)
                psT_free[gb_] = cp
            if not ln:
                xin_free[sl] = (T, T.n)
            rdy = (V, V.n)
            if on_tile is not None:
                ring_free[sl] = on_tile(
                    t, None if ring is None else
                    ring[:, sl * 1024:(sl + 1) * 1024], rdy)
        return None

    def phase_A(dst_res):
        pa = ExitStack()
        satt = ExitStack()
        sb_o, ps_o = CUR[0], CURP[0]
        CUR[0] = pa
        CURP[0] = satt
        psT2 = [ps(nm("psT"), [128, 512]) for _ in range(2)]
        psS = [ps(nm("psS"), [128, 1024]) for _ in range(2)]
        psAV = ps(nm("psAV"), [128, 512])
        psL = ps(nm("psL"), [128, 512])
        psM = psS
        psM_free = [0, 0]
        CURP[0] = pa

        maskA = sb(nm("maskA"), [128, NSLOT * 256], F32)
        for i in range(NSLOT):
            dma(pl, "gpsimd", dG, maskA[:, i * 256:(i + 1) * 256], maskc[i])
        wuk_sb = sb(nm("wuk"), [128, 2 * D], BF16)
        load_w(wuk_sb, wuk, DC, D)
        wuv_sb = sb(nm("wuv"), [128, 2 * D], BF16)
        load_w(wuv_sb, wuv, DC, D)
        wo_sb = sb(nm("wo"), [128, 8 * D], BF16)
        load_w(wo_sb, wo, D, D)
        c_kvT = sb(nm("ckvT"), [128, 2 * S], BF16)
        QT = sb(nm("QT"), [128, 8 * SQ], BF16)
        outTa = sb(nm("outTa"), [128, 8 * SQ], BF16)

        s1 = ExitStack()
        CUR[0] = s1
        wdq_sb = sb(nm("wdq"), [128, 8 * DC], BF16)
        load_w(wdq_sb, wdq, D, DC)
        wuq_sb = sb(nm("wuq"), [128, 2 * D], BF16)
        load_w(wuq_sb, wuq, DC, D)
        wload1 = dG.n
        c_qT = sb(nm("cqT"), [128, 2 * SQ], BF16)

        def own_tile(t, ring_ap, rdy):
            need("tensor", *rdy)
            need("tensor", dG, wload1)
            last = 0
            for rc in range(2):
                sl = rc
                need("tensor", V, psM_free[sl])
                def _mm(e, rc=rc, sl=sl, r=ring_ap):
                    q = None
                    for k in range(8):
                        q = e.matmul(psM[sl][0:128, 0:128],
                                     wdq_sb[:, k * DC + rc * 128:
                                            k * DC + (rc + 1) * 128],
                                     r[:, k * 128:(k + 1) * 128],
                                     start=(k == 0), stop=(k == 7))
                    return q
                tg = tick(pl, "tensor", T, _mm)
                last = tg
                need("vector", T, tg)
                psM_free[sl] = tick(pl, "vector", V,
                                    lambda e, rc=rc, t=t, sl=sl:
                                    e.tensor_copy(
                                        c_qT[:, rc * SQ + t * 128:
                                             rc * SQ + (t + 1) * 128],
                                        psM[sl][0:128, 0:128]))
            return (T, last)

        ln_stream_ring(psT2, lambda t: x_own[t * 128:(t + 1) * 128, :],
                       NSLOT, own_tile)
        for pr in range(8):
            for (noff, nw) in col_chunks(SQ):
                sl = (pr * 4 + noff // 512) % 2
                need("tensor", V, psM_free[sl])
                def _mm(e, pr=pr, noff=noff, nw=nw, sl=sl):
                    q = None
                    for rc in range(2):
                        q = e.matmul(psM[sl][0:128, 0:nw],
                                     wuq_sb[:, rc * D + pr * 128:
                                            rc * D + (pr + 1) * 128],
                                     c_qT[:, rc * SQ + noff:rc * SQ + noff + nw],
                                     start=(rc == 0), stop=(rc == 1))
                    return q
                tg = tick(pl, "tensor", T, _mm)
                need("vector", T, tg)
                psM_free[sl] = tick(pl, "vector", V,
                                    lambda e, pr=pr, noff=noff, nw=nw, sl=sl:
                                    e.tensor_copy(
                                        QT[:, pr * SQ + noff:pr * SQ + noff + nw],
                                        psM[sl][0:128, 0:nw]))
        CUR[0] = pa
        s1.close()
        bar()

        s2 = ExitStack()
        CUR[0] = s2
        wdkv_sb = sb(nm("wdkv"), [128, 8 * DC], BF16)
        load_w(wdkv_sb, wdkv, D, DC)
        wload2 = dG.n

        def full_tile(t, ring_ap, rdy):
            need("tensor", *rdy)
            need("tensor", dG, wload2)
            last = 0
            for rc in range(2):
                sl = rc
                need("tensor", V, psM_free[sl])
                def _mm(e, rc=rc, sl=sl, r=ring_ap):
                    q = None
                    for k in range(8):
                        q = e.matmul(psM[sl][0:128, 0:128],
                                     wdkv_sb[:, k * DC + rc * 128:
                                             k * DC + (rc + 1) * 128],
                                     r[:, k * 128:(k + 1) * 128],
                                     start=(k == 0), stop=(k == 7))
                    return q
                tg = tick(pl, "tensor", T, _mm)
                last = tg
                need("vector", T, tg)
                psM_free[sl] = tick(pl, "vector", V,
                                    lambda e, rc=rc, t=t, sl=sl:
                                    e.tensor_copy(
                                        c_kvT[:, rc * S + t * 128:
                                              rc * S + (t + 1) * 128],
                                        psM[sl][0:128, 0:128]))
            return (T, last)

        ln_stream_ring(psT2, lambda t: x_full[t * 128:(t + 1) * 128, :],
                       NKB, full_tile)
        CUR[0] = pa
        s2.close()
        bar()

        KTp = sb(nm("KTp"), [128, S], BF16)
        Vp = sb(nm("Vp"), [128, S], F16)
        scores_all = sb(nm("ssb"), [128, 4 * 4096], F16)
        st = sb(nm("stA"), [128, 16], F32)
        lrow_sb = sb(nm("lrow"), [1, 512], F32)
        linvB = sb(nm("linvB"), [128, 512], F32)
        onesH = sb(nm("ones1"), [128, 1], F16)
        onesR = sb(nm("onesr"), [1, 128], F32)

        def _vone(e):
            e.memset(onesH[:, :], 1.0)
            return e.memset(onesR[:, :], 1.0)
        vone = tick(pl, "vector", V, _vone)
        need("tensor", V, vone)

        need("vector", dG, dG.n)
        for hp in range(8):
            need("tensor", SC, SC.n)
            need("tensor", V, V.n)
            ktv_last = [0]
            for (noff, nw) in col_chunks(S):
                sl = (noff // 512) % 2
                need("tensor", V, psM_free[sl])
                def _mmk(e, noff=noff, nw=nw, sl=sl, hp=hp):
                    q = None
                    for rc in range(2):
                        q = e.matmul(psM[sl][0:128, 0:nw],
                                     wuk_sb[:, rc * D + hp * 128:
                                            rc * D + (hp + 1) * 128],
                                     c_kvT[:, rc * S + noff:rc * S + noff + nw],
                                     start=(rc == 0), stop=(rc == 1))
                    return q
                tg = tick(pl, "tensor", T, _mmk)
                need("vector", T, tg)
                psM_free[sl] = tick(pl, "vector", V,
                                    lambda e, noff=noff, nw=nw, sl=sl:
                                    e.tensor_copy(KTp[:, noff:noff + nw],
                                                  psM[sl][0:128, 0:nw]))
            for kb in range(NKB):
                sl = kb % 2
                need("tensor", V, psM_free[sl])
                def _mmv(e, kb=kb, sl=sl, hp=hp):
                    q = None
                    for rc in range(2):
                        q = e.matmul(psM[sl][0:128, 0:128],
                                     c_kvT[:, rc * S + kb * 128:
                                           rc * S + (kb + 1) * 128],
                                     wuv_sb[:, rc * D + hp * 128:
                                            rc * D + (hp + 1) * 128],
                                     start=(rc == 0), stop=(rc == 1))
                    return q
                tg = tick(pl, "tensor", T, _mmv)
                need("vector", T, tg)
                psM_free[sl] = tick(pl, "vector", V,
                                    lambda e, kb=kb, sl=sl:
                                    e.tensor_copy(Vp[:, kb * 128:(kb + 1) * 128],
                                                  psM[sl][0:128, 0:128]))
                ktv_last[0] = psM_free[sl]

            ktv_rdy = (V, V.n)
            need("tensor", *ktv_rdy)

            def vmap(vs):
                hh = (vs >> 1) & 1
                i = ((vs >> 2) << 1) | (vs & 1)
                return hh, i

            def qk_fn(vs, koff, scols, psum, sl, hp=hp):
                hh, i = vmap(vs)
                h0 = hh * 64
                def _q(e, i=i, koff=koff, scols=scols, psum=psum, h0=h0):
                    q = None
                    for (no, nw) in col_chunks(scols):
                        q = e.matmul(psum[0:128, no:no + nw],
                                     QT[h0:h0 + 64,
                                        hp * SQ + i * 128:
                                        hp * SQ + (i + 1) * 128],
                                     KTp[h0:h0 + 64,
                                         koff + no:koff + no + nw],
                                     start=True, stop=True)
                    return q
                tick(pl, "tensor", T, _q)

            def seg_evac(e, vs, sgi, koff, scols, psum, ssb):
                hh, i = vmap(vs)
                ext = 2 * i + 2
                moff = ext * 128 - 256 - koff
                if 0 <= moff < scols:
                    if moff > 0:
                        e.tensor_copy(ssb[:, koff:koff + moff],
                                      psum[0:128, 0:moff])
                    return e.tensor_tensor(
                        ssb[:, koff + moff:koff + moff + 256],
                        psum[0:128, moff:moff + 256],
                        maskA[:, i * 256:(i + 1) * 256],
                        op=AluOpType.add)
                return e.copy(ssb[:, koff:koff + scols],
                              psum[0:128, 0:scols])

            def seg_eng(vs, sgi):
                hh, i = vmap(vs)
                ext = 2 * i + 2
                nsg_ = ceil_div(ext * 128, 1024)
                return "vector" if sgi == nsg_ - 1 else "scalar"

            def av_lhs(kb):
                return Vp[:, kb * 128 + av_head[0] * 64:
                          kb * 128 + av_head[0] * 64 + 64]

            av_head = [0]

            def av_out(sl4, nsl):
                return psAV[av_head[0] * 64:av_head[0] * 64 + 64,
                            sl4 * 128:(sl4 + 1) * 128 + (nsl - 1) * 128]

            def evac(vs, sl4, tav, gb, hp=hp):
                hh, i = vmap(vs)
                h0 = hh * 64
                need("vector", T, tav)
                wait(pl, "vector", V, gb)
                return tick(pl, "vector", V,
                            lambda e, i=i, sl4=sl4, h0=h0:
                            e.tensor_tensor(
                                outTa[h0:h0 + 64,
                                      hp * SQ + i * 128:
                                      hp * SQ + (i + 1) * 128],
                                psAV[h0:h0 + 64,
                                     sl4 * 128:(sl4 + 1) * 128],
                                linvB[h0:h0 + 64,
                                      sl4 * 128:(sl4 + 1) * 128],
                                op=AluOpType.mult))

            def ext_fn(vs):
                hh, i = vmap(vs)
                return 2 * i + 2

            orig_av_lhs = av_lhs

            def av_lhs2(kb):
                return orig_av_lhs(kb)

            psS_free = [0, 0]
            attn_slots(psS, psS_free, psT2, psAV, psL, scores_all,
                       st, lrow_sb, linvB, onesH, onesR,
                       0.125, qk_fn, seg_evac, av_lhs2, av_out,
                       lambda: (0, av_head[0] * 64), evac, ext_fn,
                       seg_evac_eng=seg_eng, n_slots=2 * NSLOT,
                       av_head_hook=lambda i0: av_head.__setitem__(
                           0, vmap(i0)[0]))
        bar()
        satt.close()

        projst = sb(nm("projst"), [128, 8 * 512], BF16)
        xoldt = sb(nm("xold"), [128, 2 * 1024], F32)
        xnewt = sb(nm("xnew"), [128, 2 * 1024], F32)
        psW = [ps(nm("psW"), [128, 512]) for _ in range(2)]
        psR = ps(nm("psR"), [128, 1024])
        psW_free = [0, 0]
        xold_free = [0, 0]
        xnew_free = [0, 0]
        prev_tb = [0]
        prev_vn = [0]
        for q4 in range(4):
            pj_ticks = []
            for c in range(8):
                sl = c % 2
                need("tensor", SC, psW_free[sl])
                def _mw(e, q4=q4, c=c, sl=sl):
                    q = None
                    for c2 in range(8):
                        q = e.matmul(psW[sl][0:128, 0:512],
                                     wo_sb[:, c2 * D + c * 128:
                                           c2 * D + (c + 1) * 128],
                                     outTa[:, c2 * SQ + q4 * 512:
                                           c2 * SQ + (q4 + 1) * 512],
                                     start=(c2 == 0), stop=(c2 == 7))
                    return q
                tg = tick(pl, "tensor", T, _mw)
                need("scalar", T, tg)
                need("scalar", T, prev_tb[0])
                sgc = tick(pl, "scalar", SC, lambda e, c=c, sl=sl:
                           e.copy(projst[:, c * 512:(c + 1) * 512],
                                  psW[sl][0:128, 0:512]))
                psW_free[sl] = sgc
                pj_ticks.append(sgc)
            for tt4 in range(4):
                t = q4 * 4 + tt4
                sl = t % 2
                need("sync", V, xold_free[sl])
                ld = dma(pl, "sync", dO[sl], xoldt[:, sl * 1024:(sl + 1) * 1024],
                         x_own[t * 128:(t + 1) * 128, :])
                need("tensor", SC, pj_ticks[-1])
                need("tensor", V, prev_vn[0])
                def _tb(e, tt4=tt4):
                    q = None
                    for c in range(8):
                        q = e.matmul(psR[:, c * 128:(c + 1) * 128],
                                     projst[:, c * 512 + tt4 * 128:
                                            c * 512 + (tt4 + 1) * 128],
                                     identB[:, :], start=True, stop=True)
                    return q
                tg = tick(pl, "tensor", T, _tb)
                prev_tb[0] = tg
                need("vector", T, tg)
                need("vector", dO[sl], ld)
                need("vector", dN[sl], xnew_free[sl])
                vn = tick(pl, "vector", V, lambda e, sl=sl:
                          e.tensor_tensor(xnewt[:, sl * 1024:(sl + 1) * 1024],
                                          xoldt[:, sl * 1024:(sl + 1) * 1024],
                                          psR[:, :], op=AluOpType.add))
                xold_free[sl] = vn
                prev_vn[0] = vn
                need("sync", V, vn)
                xnew_free[sl] = dma(pl, "sync", dN[sl],
                                    dst_res[t * 128:(t + 1) * 128, :],
                                    xnewt[:, sl * 1024:(sl + 1) * 1024])
        bar()
        CUR[0] = sb_o
        CURP[0] = ps_o
        pa.close()

    def phase_ffn(src_res, dst_res):
        pf = ExitStack()
        sb_o, ps_o = CUR[0], CURP[0]
        CUR[0] = pf
        CURP[0] = pf
        psT2 = [ps(nm("psT"), [128, 512]) for _ in range(2)]
        psH = [ps(nm("psH"), [128, 512]) for _ in range(3)]
        psO = [ps(nm("psO"), [128, 512]) for _ in range(3)]
        psH_free = [(SC, 0), (SC, 0), (SC, 0)]
        psO_free = [0, 0, 0]

        xT = sb(nm("xT"), [128, 8 * SQ], BF16)
        b1c = sb(nm("b1c"), [128, 128], F32)
        b1n = sb(nm("b1n"), [8, HID], BF16)
        b2sb = sb(nm("b2sb"), [8, D], BF16)
        wredsb = sb(nm("wred"), [128, 8 * GD], BF16)
        wgf = sb(nm("wgf"), [8, 64], F32)
        wgnb = sb(nm("wgnb"), [8, GD], BF16)
        wgT = sb(nm("wgT"), [16, 8], BF16)
        xredT = sb(nm("xredT"), [16, SQ], BF16)
        lgT = sb(nm("lgT"), [8, SQ], BF16)
        ge8 = sb(nm("ge8"), [128, 8], F32)
        gsc = sb(nm("gsc"), [128, 4], F32)
        ohs = sb(nm("ohs"), [128, NSLOT * 8], F32)
        ohb = sb(nm("ohb"), [128, 8], BF16)
        ohsT = sb(nm("ohsT"), [8, SQ], BF16)

        dma(pl, "gpsimd", dG, b1n[0:8, :], b1m)
        dma(pl, "gpsimd", dG, b2sb[0:8, :], b2m)
        for k in range(8):
            dma(pl, "gpsimd", dG, wredsb[:, k * GD:(k + 1) * GD],
                wred[k * 128:(k + 1) * 128, :])
        ldwg = dma(pl, "sync", dW, wgf[0:8, 0:GD], wgm)
        cload = dG.n

        _lns = ExitStack()
        CUR[0] = _lns
        ln_stream_ring(psT2, lambda t: src_res[t * 128:(t + 1) * 128, :],
                       NSLOT, dst=xT)
        CUR[0] = pf
        bar()
        _lns.close()
        o_acc = sb(nm("oacc"), [128, 16 * 1024], F32)
        _exps = ExitStack()
        CUR[0] = _exps
        hT = sb(nm("hT"), [128, 4 * SQ], BF16)
        w1sb = [sb(nm("w1sb"), [128, 8 * HID], BF16) for _ in range(2)]
        w2sb = sb(nm("w2sb"), [128, 4 * 1024], BF16)
        CUR[0] = pf

        need("tensor", dG, cload)
        b1_ticks = []
        for c in range(16):
            sl = c % 2
            need("tensor", *psH_free[sl])
            tg = tick(pl, "tensor", T, lambda e, c=c, sl=sl:
                      e.matmul(psH[sl][0:128, 0:8],
                               b1n[0:8, c * 128:(c + 1) * 128],
                               identB[0:8, 0:8], start=True, stop=True))
            need("scalar", T, tg)
            sgc = tick(pl, "scalar", SC, lambda e, c=c, sl=sl:
                       e.copy(b1c[:, c * 8:(c + 1) * 8],
                              psH[sl][0:128, 0:8]))
            psH_free[sl] = (SC, sgc)
            b1_ticks.append(sgc)

        need("vector", dW, ldwg)
        vw0 = tick(pl, "vector", V, lambda e:
                   e.tensor_tensor(wgf[0:8, 16:32], wgf[0:8, 0:16],
                                   wgf[0:8, 0:16], op=AluOpType.mult))
        wait(pl, "vector", V, vw0)
        v1 = tick(pl, "vector", V, lambda e:
                  e.reduce_sum(wgf[0:8, 32:33], wgf[0:8, 16:32], axis=AX))
        need("scalar", V, v1)
        need("scalar", V, konst_rdy)
        s1 = tick(pl, "scalar", SC, lambda e:
                  e.activation(wgf[0:8, 33:34], wgf[0:8, 32:33], ACT.Sqrt,
                               bias=konst[0:8, 0:1]))
        need("vector", SC, s1)
        vwa = tick(pl, "vector", V, lambda e:
                   e.tensor_scalar_max(wgf[0:8, 34:35], wgf[0:8, 33:34],
                                       1e-4))
        wait(pl, "vector", V, vwa)
        vwb = tick(pl, "vector", V, lambda e:
                   e.reciprocal(wgf[0:8, 35:36], wgf[0:8, 34:35]))
        wait(pl, "vector", V, vwb)
        vwc = tick(pl, "vector", V, lambda e:
                   e.tensor_scalar(wgf[0:8, 36:52], wgf[0:8, 0:16],
                                   wgf[0:8, 35:36], None,
                                   op0=AluOpType.mult))
        wait(pl, "vector", V, vwc)
        v2 = tick(pl, "vector", V, lambda e:
                  e.tensor_copy(wgnb[0:8, :], wgf[0:8, 36:52]))
        need("tensor", V, v2)
        need("tensor", *psH_free[0])
        tg = tick(pl, "tensor", T, lambda e:
                  e.matmul(psH[0][0:16, 0:8], wgnb[0:8, :],
                           identB[0:8, 0:8], start=True, stop=True))
        need("vector", T, tg)
        vwgT = tick(pl, "vector", V, lambda e:
                    e.tensor_copy(wgT[0:16, :], psH[0][0:16, 0:8]))
        psH_free[0] = (V, vwgT)
        for (noff, nw) in col_chunks(SQ):
            sl = (noff // 512) % 2
            need("tensor", *psH_free[sl])
            def _mx(e, noff=noff, nw=nw, sl=sl):
                q = None
                for k in range(8):
                    q = e.matmul(psH[sl][0:16, 0:nw],
                                 wredsb[:, k * GD:(k + 1) * GD],
                                 xT[:, k * SQ + noff:k * SQ + noff + nw],
                                 start=(k == 0), stop=(k == 7))
                return q
            tg = tick(pl, "tensor", T, _mx)
            need("vector", T, tg)
            psH_free[sl] = (V, tick(pl, "vector", V,
                                lambda e, noff=noff, nw=nw, sl=sl:
                                e.tensor_copy(xredT[0:16, noff:noff + nw],
                                              psH[sl][0:16, 0:nw])))
        for (noff, nw) in col_chunks(SQ):
            sl = (noff // 512) % 2
            need("tensor", V, psO_free[sl])
            def _ml(e, noff=noff, nw=nw, sl=sl):
                return e.matmul(psO[sl][0:8, 0:nw], wgT[0:16, :],
                                xredT[0:16, noff:noff + nw],
                                start=True, stop=True)
            tg = tick(pl, "tensor", T, _ml)
            need("vector", T, tg)
            psO_free[sl] = tick(pl, "vector", V,
                                lambda e, noff=noff, nw=nw, sl=sl:
                                e.tensor_copy(lgT[0:8, noff:noff + nw],
                                              psO[sl][0:8, 0:nw]))
        prev_tg2 = [0]
        for t in range(NSLOT):
            sl = t % 2
            need("tensor", *psH_free[sl])
            tg = tick(pl, "tensor", T, lambda e, t=t, sl=sl:
                      e.matmul(psH[sl][0:128, 0:8],
                               lgT[0:8, t * 128:(t + 1) * 128],
                               identB[0:8, 0:8], start=True, stop=True))
            need("vector", T, tg)
            vg0 = tick(pl, "vector", V, lambda e, sl=sl:
                       e.reduce_max(gsc[:, 0:1], psH[sl][0:128, 0:8],
                                    axis=AX))
            wait(pl, "vector", V, vg0)
            v1 = tick(pl, "vector", V, lambda e:
                      e.tensor_scalar_mul(gsc[:, 1:2], gsc[:, 0:1], -1.0))
            need("scalar", V, v1)
            s1 = tick(pl, "scalar", SC, lambda e, sl=sl:
                      e.activation(ge8[:, :], psH[sl][0:128, 0:8], ACT.Exp,
                                   bias=gsc[:, 1:2]))
            need("vector", SC, s1)
            need("vector", T, prev_tg2[0])
            vgl = tick(pl, "vector", V, lambda e:
                       e.reduce_sum(gsc[:, 2:3], ge8[:, :], axis=AX))
            wait(pl, "vector", V, vgl)
            v2a__ = tick(pl, "vector", V, lambda e, t=t, sl=sl:
                         e.reciprocal(gsc[:, 3:4], gsc[:, 2:3]))
            wait(pl, "vector", V, v2a__)
            v2b__ = tick(pl, "vector", V, lambda e, t=t, sl=sl:
                         e.tensor_scalar(ohs[:, t * 8:(t + 1) * 8],
                                         psH[sl][0:128, 0:8], gsc[:, 0:1],
                                         None, op0=AluOpType.is_equal))
            wait(pl, "vector", V, v2b__)
            v2c__ = tick(pl, "vector", V, lambda e, t=t:
                         e.tensor_scalar(ohs[:, t * 8:(t + 1) * 8],
                                         ohs[:, t * 8:(t + 1) * 8],
                                         gsc[:, 3:4], None,
                                         op0=AluOpType.mult))
            wait(pl, "vector", V, v2c__)
            v2 = tick(pl, "vector", V, lambda e, t=t:
                      e.tensor_copy(ohb[:, :], ohs[:, t * 8:(t + 1) * 8]))
            psH_free[sl] = (V, v2)
            need("tensor", V, v2)
            need("tensor", V, psO_free[sl])
            tg2 = tick(pl, "tensor", T, lambda e, sl=sl:
                       e.matmul(psO[sl][0:8, 0:128], ohb[:, :],
                                identB[:, :], start=True, stop=True))
            prev_tg2[0] = tg2
            need("vector", T, tg2)
            vT = tick(pl, "vector", V, lambda e, t=t, sl=sl:
                      e.tensor_copy(ohsT[0:8, t * 128:(t + 1) * 128],
                                    psO[sl][0:8, 0:128]))
            psO_free[sl] = vT
        for t in range(NSLOT):
            for do in range(2):
                sl = (t * 2 + do) % 2
                need("tensor", *psH_free[sl])
                tg = tick(pl, "tensor", T, lambda e, t=t, do=do, sl=sl:
                          e.matmul(psH[sl][0:128, 0:512],
                                   ohsT[0:8, t * 128:(t + 1) * 128],
                                   b2sb[0:8, do * 512:(do + 1) * 512],
                                   start=True, stop=True))
                need("vector", T, tg)
                psH_free[sl] = (V, tick(
                    pl, "vector", V, lambda e, t=t, do=do, sl=sl:
                    e.tensor_copy(
                        o_acc[:, t * 1024 + do * 512:
                              t * 1024 + (do + 1) * 512],
                        psH[sl][0:128, 0:512])))

        w1_free = [(T, 0), (T, 0)]
        w1_rdy = {}
        w2_free = [0]
        prev_oT = [0]

        def load_w1(e_i):
            wb = e_i % 2
            need("gpsimd", *w1_free[wb])
            for d_ in range(8):
                tgt = dma(pl, "gpsimd", dW1[wb],
                          w1sb[wb][:, d_ * HID:(d_ + 1) * HID],
                          W1[e_i, d_ * 128:(d_ + 1) * 128, :])
            w1_rdy[e_i] = (dW1[wb], tgt)

        load_w1(0)
        for e_i in range(E):
            if e_i + 1 < E:
                load_w1(e_i + 1)
            wb = e_i % 2
            last_mm = 0
            for qr in range(4):
                need("gpsimd", T, w2_free[0])
                for k in range(4):
                    w2ld = dma(pl, "gpsimd", dW2, w2sb[:, k * 1024:(k + 1) * 1024],
                               W2[e_i, qr * 512 + k * 128:
                                  qr * 512 + (k + 1) * 128, :])
                h_ticks = []
                for hc in range(4):
                    cg = qr * 4 + hc
                    for t4 in range(4):
                        psl = (hc * 4 + t4) % 3
                        need("tensor", *w1_rdy[e_i])
                        need("tensor", *psH_free[psl])
                        def _mh(e, wb=wb, qr=qr, hc=hc, t4=t4, psl=psl):
                            q = None
                            for k in range(8):
                                q = e.matmul(psH[psl][0:128, 0:512],
                                             w1sb[wb][:, k * HID +
                                                      qr * 512 + hc * 128:
                                                      k * HID +
                                                      qr * 512 +
                                                      (hc + 1) * 128],
                                             xT[:, k * SQ + t4 * 512:
                                                k * SQ + (t4 + 1) * 512],
                                             start=(k == 0), stop=(k == 7))
                            return q
                        tg = tick(pl, "tensor", T, _mh)
                        last_mm = tg
                        need("scalar", T, tg)
                        need("scalar", T, prev_oT[0])
                        sgc = tick(pl, "scalar", SC,
                                   lambda e, hc=hc, t4=t4, psl=psl, cg=cg,
                                   e_i=e_i:
                                   e.activation(
                                       hT[:, hc * SQ + t4 * 512:
                                          hc * SQ + (t4 + 1) * 512],
                                       psH[psl][0:128, 0:512], ACT.Gelu,
                                       bias=b1c[:, cg * 8 + e_i:
                                                cg * 8 + e_i + 1]))
                        psH_free[psl] = (SC, sgc)
                        h_ticks.append(sgc)
                need("tensor", dW2, w2ld)
                need("tensor", SC, h_ticks[-1])
                for tt in range(NSLOT):
                    for do in range(2):
                        sl = (tt * 2 + do) % 3
                        need("tensor", V, psO_free[sl])
                        def _mo(e, tt=tt, do=do, sl=sl):
                            q = None
                            for k in range(4):
                                q = e.matmul(psO[sl][0:128, 0:512],
                                             hT[:, k * SQ + tt * 128:
                                                k * SQ + (tt + 1) * 128],
                                             w2sb[:, k * 1024 + do * 512:
                                                  k * 1024 + (do + 1) * 512],
                                             start=(k == 0), stop=(k == 3))
                            return q
                        tg = tick(pl, "tensor", T, _mo)
                        prev_oT[0] = tg
                        need("vector", T, tg)
                        psO_free[sl] = tick(
                            pl, "vector", V,
                            lambda e, tt=tt, do=do, sl=sl, e_i=e_i:
                            e.scalar_tensor_tensor(
                                o_acc[:, tt * 1024 + do * 512:
                                      tt * 1024 + (do + 1) * 512],
                                psO[sl][0:128, 0:512],
                                ohs[:, tt * 8 + e_i:tt * 8 + e_i + 1],
                                o_acc[:, tt * 1024 + do * 512:
                                      tt * 1024 + (do + 1) * 512],
                                op0=AluOpType.mult, op1=AluOpType.add))
                w2_free[0] = prev_oT[0]
            w1_free[wb] = (T, last_mm)

        bar()
        _exps.close()
        xoldt = sb(nm("xold"), [128, 2 * 1024], F32)
        xnewt = sb(nm("xnew"), [128, 2 * 1024], F32)
        xold_free = [0, 0]
        xnew_free = [0, 0]
        for t in range(NSLOT):
            sl = t % 2
            need("sync", V, xold_free[sl])
            ld = dma(pl, "sync", dO[sl], xoldt[:, sl * 1024:(sl + 1) * 1024],
                     src_res[t * 128:(t + 1) * 128, :])
            need("vector", dO[sl], ld)
            need("vector", dN[sl], xnew_free[sl])
            vn = tick(pl, "vector", V, lambda e, t=t, sl=sl:
                      e.tensor_tensor(xnewt[:, sl * 1024:(sl + 1) * 1024],
                                      xoldt[:, sl * 1024:(sl + 1) * 1024],
                                      o_acc[:, t * 1024:(t + 1) * 1024],
                                      op=AluOpType.add))
            xold_free[sl] = vn
            need("sync", V, vn)
            xnew_free[sl] = dma(pl, "sync", dN[sl],
                                dst_res[t * 128:(t + 1) * 128, :],
                                xnewt[:, sl * 1024:(sl + 1) * 1024])
        bar()
        CUR[0] = sb_o
        CURP[0] = ps_o
        pf.close()

    def phase_C(src_res, dst_res):
        pc = ExitStack()
        satt = ExitStack()
        sb_o, ps_o = CUR[0], CURP[0]
        CUR[0] = pc
        CURP[0] = satt
        psT2 = [ps(nm("psT"), [128, 512]) for _ in range(2)]
        psS = [ps(nm("psS"), [128, 1024]) for _ in range(2)]
        psAV = ps(nm("psAV"), [128, 512])
        psL = ps(nm("psL"), [128, 512])
        psRaw = psL
        psM = psS
        psM_free = [0, 0]
        CURP[0] = pc

        xnT3 = sb(nm("xnT3"), [128, 8 * S], BF16)
        QTb = sb(nm("QTb"), [128, 8 * SQ], BF16)
        rebc = sb(nm("rebc"), [128, 8], F32)
        rpbc = sb(nm("rpbc"), [128, 8], F32)
        rbn = sb(nm("rbn"), [8, 2 * DB], BF16)
        dma(pl, "gpsimd", dH, rbn[0:8, 0:DB], rebm)
        rbload = dma(pl, "gpsimd", dH, rbn[0:8, DB:2 * DB], rpbm)

        need("tensor", dH, rbload)
        tg = tick(pl, "tensor", T, lambda e:
                  e.matmul(psRaw[0:128, 0:8], rbn[0:8, 0:DB],
                           identB[0:8, 0:8], start=True, stop=True))
        need("vector", T, tg)
        v_ = tick(pl, "vector", V, lambda e:
                  e.tensor_copy(rebc[:, :], psRaw[0:128, 0:8]))
        need("tensor", V, v_)
        tg = tick(pl, "tensor", T, lambda e:
                  e.matmul(psRaw[0:128, 0:8], rbn[0:8, DB:2 * DB],
                           identB[0:8, 0:8], start=True, stop=True))
        need("vector", T, tg)
        v2_ = tick(pl, "vector", V, lambda e:
                   e.tensor_copy(rpbc[:, :], psRaw[0:128, 0:8]))
        need("tensor", V, v2_)

        s1 = ExitStack()
        CUR[0] = s1
        wqs_sb = sb(nm("wqs"), [128, 8 * D], BF16)
        wqload = load_w(wqs_sb, wqs, D, D)

        def qtb_tile(t, ring_ap, rdy):
            need("tensor", *rdy)
            need("tensor", dG, wqload)
            last = 0
            for pr in range(8):
                sl = pr % 2
                need("tensor", V, psM_free[sl])
                def _mm(e, pr=pr, sl=sl, r=ring_ap):
                    q = None
                    for k in range(8):
                        q = e.matmul(psM[sl][0:128, 0:128],
                                     wqs_sb[:, k * D + pr * 128:
                                            k * D + (pr + 1) * 128],
                                     r[:, k * 128:(k + 1) * 128],
                                     start=(k == 0), stop=(k == 7))
                    return q
                tg = tick(pl, "tensor", T, _mm)
                last = tg
                need("vector", T, tg)
                psM_free[sl] = tick(pl, "vector", V,
                                    lambda e, pr=pr, t=t, sl=sl:
                                    e.tensor_copy(
                                        QTb[:, pr * SQ + t * 128:
                                            pr * SQ + (t + 1) * 128],
                                        psM[sl][0:128, 0:128]))
            return (T, last)

        ln_stream_ring(psT2, lambda t: src_res[t * 128:(t + 1) * 128, :],
                       NSLOT, qtb_tile,
                       bounce=lambda t: ccin.ap()[t * 128:(t + 1) * 128, :])
        CUR[0] = pc
        s1.close()
        bar()

        need("gpsimd", dW, dW.n)
        def _cc(e):
            return e.collective_compute(
                "AllGather", mybir.AluOpType.bypass,
                replica_groups=(cc_groups if cc_groups is not None else
                                [[0, 1], [2, 3], [4, 5], [6, 7]]),
                ins=[ccin.ap().opt()], outs=[ccout.ap().opt()])
        tick(pl, "gpsimd", CCs, _cc)
        ccdone = CCs.n
        need("sync", CCs, ccdone)

        ln_stream_ring(psT2, lambda t: ccout.ap()[t * 128:(t + 1) * 128, :],
                       NKB, dst=xnT3, ln=False, src_f32=False,
                       dst_colmap=_gath_global_block)
        bar()

        if DEBUG:
            dma(pl, "gpsimd", dG, dbg_xres, src_res)
            dma(pl, "gpsimd", dG, dbg_ccout, ccout.ap())
        shd = ExitStack()
        CUR[0] = shd
        KTh = sb(nm("KTh"), [128, S], BF16)
        Vh = sb(nm("Vh"), [128, S], F16)
        relkT = sb(nm("relkT"), [128, POSW - 128], BF16)
        qreb = sb(nm("qreb"), [128, 2 * 128], BF16)
        qrpb = sb(nm("qrpb"), [128, 2 * 128], BF16)
        scores_all = sb(nm("ssb"), [128, 4 * 4096], F16)
        stC = sb(nm("stC"), [128, 16], F32)
        lrow_sb = sb(nm("lrow"), [1, 512], F32)
        linvB = sb(nm("linvB"), [128, 512], F32)
        onesH = sb(nm("ones1"), [128, 1], F16)
        onesR = sb(nm("onesr"), [1, 128], F32)

        def _vone(e):
            e.memset(onesH[:, :], 1.0)
            return e.memset(onesR[:, :], 1.0)
        vone = tick(pl, "vector", V, _vone)
        need("tensor", V, vone)
        diag2 = [sb(nm("diag"), [128, 4096], F16) for _ in range(2)]
        rawf = sb(nm("rawf"), [128, RWIN], F16)
        outTbst = sb(nm("outTbst"), [128, 2 * 512], BF16)
        wks_h = sb(nm("wksh"), [128, 8 * 128], BF16)
        wvs_h = sb(nm("wvsh"), [128, 8 * 128], BF16)
        wrel_h = sb(nm("wrelh"), [128, 2 * 128], BF16)
        posb_a = sb(nm("posba"), [128, 512], BF16)
        posb_b = sb(nm("posbb"), [128, 512], BF16)
        zpad = tick(pl, "vector", V, lambda e:
                    e.memset(posb_b[:, :], 0.0))
        wrelz = [0]

        diag_rdy = {}
        raw_free = [(dRW, 0)]
        raw_sc = [0]
        last_diag_read = [0, 0]
        diag_free = [(V, 0), (V, 0)]
        spill_tgt = [0]
        spill_grp = [0, 0]

        for h in range(HB):
            need("tensor", SC, SC.n)
            need("tensor", V, V.n)
            need("vector", dG, spill_tgt[0])
            need("gpsimd", T, T.n)
            for k in range(8):
                dma(pl, "gpsimd", dH, wks_h[:, k * 128:(k + 1) * 128],
                    wks[k * 128:(k + 1) * 128, h * 128:(h + 1) * 128])
                dma(pl, "gpsimd", dH, wvs_h[:, k * 128:(k + 1) * 128],
                    wvs[k * 128:(k + 1) * 128, h * 128:(h + 1) * 128])
            if h == 0:
                wrelz[0] = tick(pl, "vector", V, lambda e:
                                e.memset(wrel_h[:, 128:256], 0.0))
            need("gpsimd", V, wrelz[0])
            dma(pl, "gpsimd", dH, wrel_h[0:128, 0:128],
                wrel[0:128, h * 128:(h + 1) * 128])
            whload = dma(pl, "gpsimd", dH, wrel_h[0:64, 128:256],
                         wrel[128:192, h * 128:(h + 1) * 128])
            scl = 1.0 / math.sqrt(DB)
            vq = V.n
            qrpb_state = {"free": (T, 0)}
            qreb_state = {0: (T, 0), 1: (T, 0)}
            for (noff, nw) in col_chunks(S):
                sl = (noff // 512) % 2
                need("tensor", V, psM_free[sl])
                need("tensor", dH, whload)
                def _mk(e, noff=noff, nw=nw, sl=sl):
                    q = None
                    for k in range(8):
                        q = e.matmul(psM[sl][0:128, 0:nw],
                                     wks_h[:, k * 128:(k + 1) * 128],
                                     xnT3[:, k * S + noff:k * S + noff + nw],
                                     start=(k == 0), stop=(k == 7))
                    return q
                tg = tick(pl, "tensor", T, _mk)
                need("vector", T, tg)
                psM_free[sl] = tick(pl, "vector", V,
                                    lambda e, noff=noff, nw=nw, sl=sl:
                                    e.tensor_copy(KTh[:, noff:noff + nw],
                                                  psM[sl][0:128, 0:nw]))
            for kb in range(NKB):
                sl = kb % 2
                need("tensor", V, psM_free[sl])
                def _mv(e, kb=kb, sl=sl):
                    q = None
                    for k in range(8):
                        q = e.matmul(psM[sl][0:128, 0:128],
                                     xnT3[:, k * S + kb * 128:
                                          k * S + (kb + 1) * 128],
                                     wvs_h[:, k * 128:(k + 1) * 128],
                                     start=(k == 0), stop=(k == 7))
                    return q
                tg = tick(pl, "tensor", T, _mv)
                need("vector", T, tg)
                psM_free[sl] = tick(pl, "vector", V,
                                    lambda e, kb=kb, sl=sl:
                                    e.tensor_copy(Vh[:, kb * 128:(kb + 1) * 128],
                                                  psM[sl][0:128, 0:128]))
            pos_free = [(T, 0), (T, 0)]
            for (noff, nw) in col_chunks(POSW - 128):
                sl = (noff // 512) % 2
                need("gpsimd", T, pos_free[0][1])
                need("gpsimd", V, zpad)
                dma(pl, "gpsimd", dP[0], posb_a[:, 0:nw],
                    posc[0:128, 128 + noff:128 + noff + nw])
                pld = dma(pl, "gpsimd", dP[0],
                          posb_b[0:64, 0:nw],
                          posc[128:192, 128 + noff:128 + noff + nw])
                need("tensor", dP[0], pld)
                need("tensor", V, psM_free[sl])
                def _mr(e, noff=noff, nw=nw, sl=sl):
                    e.matmul(psM[sl][0:128, 0:nw], wrel_h[0:128, 0:128],
                             posb_a[:, 0:nw],
                             start=True, stop=False)
                    return e.matmul(psM[sl][0:128, 0:nw],
                                    wrel_h[0:128, 128:256],
                                    posb_b[0:128, 0:nw],
                                    start=False, stop=True)
                tg = tick(pl, "tensor", T, _mr)
                pos_free[0] = (T, tg)
                need("vector", T, tg)
                psM_free[sl] = tick(pl, "vector", V,
                                    lambda e, noff=noff, nw=nw, sl=sl:
                                    e.tensor_copy(relkT[:, noff:noff + nw],
                                                  psM[sl][0:128, 0:nw]))

            def prep_fn(i, h=h):
                par = i % 2
                qsl = i % 2
                wstart = 3968 - 256 * i
                need("vector", *qrpb_state["free"])
                vqp = tick(pl, "vector", V, lambda e, i=i, h=h, qsl=qsl:
                           e.tensor_scalar(qrpb[:, qsl * 128:(qsl + 1) * 128],
                                           QTb[:, h * SQ + i * 128:
                                               h * SQ + (i + 1) * 128],
                                           scl, rpbc[:, h:h + 1],
                                           op0=AluOpType.mult,
                                           op1=AluOpType.add))
                need("tensor", V, vqp)
                need("scalar", *raw_free[0])
                last_rm = 0
                for ci, (noff, nw) in enumerate(col_chunks(RWIN)):
                    sl_ = ci % 2
                    if isinstance(psS_free[sl_], tuple):
                        need("tensor", *psS_free[sl_])
                    def _rm(e, i=i, noff=noff, nw=nw, ws=wstart - 128,
                            qsl=qsl, sl_=sl_):
                        return e.matmul(psS[sl_][0:128, 0:nw],
                                        qrpb[:, qsl * 128:(qsl + 1) * 128],
                                        relkT[:, ws + noff:ws + noff + nw],
                                        start=True, stop=True)
                    tg = tick(pl, "tensor", T, _rm)
                    last_rm = tg
                    need("scalar", T, tg)
                    raw_sc[0] = tick(pl, "scalar", SC,
                                     lambda e, noff=noff, nw=nw, sl_=sl_:
                                     e.copy(rawf[:, noff:noff + nw],
                                            psS[sl_][0:128, 0:nw]))
                    psS_free[sl_] = (SC, raw_sc[0])
                qrpb_state["free"] = (T, last_rm)
                last_sc = raw_sc[0]
                need("sync", SC, last_sc)
                need("sync", dD[par], last_diag_read[par])
                base = relscr[par]
                w2d = base[0:128 * RPITCH].rearrange("(p j) -> p j", j=RPITCH)
                wt = dma(pl, "sync", dRW, w2d[:, 0:RWIN], rawf[:, :])
                raw_free[0] = (dRW, wt)
                need("scalar", dRW, wt)
                need("scalar", *diag_free[i % 2])
                dv = base[127:127 + 128 * RWIN].rearrange("(p k) -> p k",
                                                          k=RWIN)
                dt = dma(pl, "scalar", dD[par], diag2[i % 2][:, :],
                         dv[:, 0:4096])
                last_diag_read[par] = dt
                diag_rdy[i] = (dD[par], dt)

            def qk_fn(i, koff, scols, psum, sl, h=h):
                qsl = i % 2
                if koff == 0:
                    need("vector", *qreb_state[qsl])
                    vqb = tick(pl, "vector", V, lambda e, i=i, h=h, qsl=qsl:
                               e.tensor_scalar(
                                   qreb[:, qsl * 128:(qsl + 1) * 128],
                                   QTb[:, h * SQ + i * 128:
                                       h * SQ + (i + 1) * 128],
                                   scl, rebc[:, h:h + 1],
                                   op0=AluOpType.mult, op1=AluOpType.add))
                    need("tensor", V, vqb)
                def _q(e, i=i, koff=koff, scols=scols, psum=psum, qsl=qsl):
                    q = None
                    for (no, nw) in col_chunks(scols):
                        q = e.matmul(psum[0:128, no:no + nw],
                                     qreb[:, qsl * 128:(qsl + 1) * 128],
                                     KTh[:, koff + no:koff + no + nw],
                                     start=True, stop=True)
                    return q
                tick(pl, "tensor", T, _q)
                qreb_state[qsl] = (T, T.n)

            def seg_evac(e, i, sgi, koff, scols, psum, ssb):
                return e.tensor_tensor(ssb[:, koff:koff + scols],
                                       psum[0:128, 0:scols],
                                       diag2[i % 2][:, koff:koff + scols],
                                       op=AluOpType.add)

            def av_lhs(kb):
                return Vh[:, kb * 128:(kb + 1) * 128]

            def av_out(sl4, nsl):
                return psAV[0:128, sl4 * 128:(sl4 + nsl) * 128]

            def evac(i, sl4, tav, gb, h=h):
                need("vector", T, tav)
                wait(pl, "vector", V, gb)
                g2 = (i // 4) % 2
                need("vector", dSp[g2], spill_grp[g2])
                vt = tick(pl, "vector", V,
                          lambda e, i=i, sl4=sl4, g2=g2:
                          e.tensor_tensor(
                              outTbst[:, g2 * 512 + (i % 4) * 128:
                                      g2 * 512 + (i % 4 + 1) * 128],
                              psAV[0:128, sl4 * 128:(sl4 + 1) * 128],
                              linvB[:, sl4 * 128:(sl4 + 1) * 128],
                              op=AluOpType.mult))
                diag_free[i % 2] = (V, vt)
                if i % 4 == 3:
                    need("gpsimd", V, vt)
                    spill_grp[g2] = dma(
                        pl, "gpsimd", dSp[g2],
                        outbsp[h][:, (i - 3) * 128:(i + 1) * 128],
                        outTbst[:, g2 * 512:(g2 + 1) * 512])
                return vt

            if DEBUG and h == 0:
                need("gpsimd", V, V.n)
                dma(pl, "gpsimd", dG, dbg_hd[0][:, 0:SQ], qreb[:, :])
                dma(pl, "gpsimd", dG, dbg_hd[0][:, SQ:2 * SQ],
                    QTb[:, 0:SQ])
                dma(pl, "gpsimd", dG, dbg_hd[1], KTh[:, :])
                dma(pl, "gpsimd", dG, dbg_hd[2], Vh[:, :])
                dma(pl, "gpsimd", dG, dbg_hd[3][:, 0:8], rebc[:, :])
                dma(pl, "gpsimd", dG, dbg_hd[3][:, 8:16], rpbc[:, :])
                dma(pl, "gpsimd", dG, dbg_hd[3][0:8, 16:16 + 2 * DB],
                    rbn[0:8, :])
            psS_free = [0, 0]
            attn_slots(psS, psS_free, psT2, psAV, psL, scores_all,
                       stC, lrow_sb, linvB, onesH, onesR,
                       1.0, qk_fn, seg_evac, av_lhs, av_out,
                       None, evac, lambda i: NKB,
                       diag_rdy_fn=lambda i: diag_rdy[i],
                       prep_fn=prep_fn)
            spill_tgt[0] = spill_grp[0]
        if DEBUG:
            for hh_ in range(HB):
                dma(pl, "gpsimd", dG, dbg_outb[hh_], outbsp[hh_])
        bar()
        CUR[0] = pc
        shd.close()
        satt.close()

        outTb = sb(nm("outTb"), [128, 8 * SQ], BF16)
        wout_sb = sb(nm("wout"), [128, 8 * D], BF16)
        boutc = sb(nm("boutc"), [128, 8], F32)
        bon = sb(nm("bon"), [8, 128], BF16)
        projst = sb(nm("projst"), [128, 8 * 512], BF16)
        xoldt = sb(nm("xold"), [128, 2 * 1024], F32)
        xnewt = sb(nm("xnew"), [128, 2 * 1024], F32)
        psW = [ps(nm("psW"), [128, 512]) for _ in range(2)]
        psR = ps(nm("psR"), [128, 1024])
        load_w(wout_sb, wout, D, D)
        for h in range(HB):
            dma(pl, "gpsimd", dG, outTb[:, h * SQ:(h + 1) * SQ], outbsp[h])
        bload = dma(pl, "gpsimd", dG, bon[0:8, :],
                    boutv.rearrange("(c p) -> c p", p=128))
        need("tensor", dG, bload)
        tg = tick(pl, "tensor", T, lambda e:
                  e.matmul(psR[0:128, 0:8], bon[0:8, :],
                           identB[0:8, 0:8], start=True, stop=True))
        need("vector", T, tg)
        vb = tick(pl, "vector", V, lambda e:
                  e.tensor_copy(boutc[:, :], psR[0:128, 0:8]))
        psW_free = [0, 0]
        xold_free = [0, 0]
        xnew_free = [0, 0]
        prev_tb = [0]
        prev_vn = [0]
        need("tensor", V, vb)
        for q4 in range(4):
            pj_ticks = []
            for c in range(8):
                sl = c % 2
                need("tensor", V, psW_free[sl])
                def _mw(e, q4=q4, c=c, sl=sl):
                    q = None
                    for c2 in range(8):
                        q = e.matmul(psW[sl][0:128, 0:512],
                                     wout_sb[:, c2 * D + c * 128:
                                             c2 * D + (c + 1) * 128],
                                     outTb[:, c2 * SQ + q4 * 512:
                                           c2 * SQ + (q4 + 1) * 512],
                                     start=(c2 == 0), stop=(c2 == 7))
                    return q
                tg = tick(pl, "tensor", T, _mw)
                need("vector", T, tg)
                need("vector", T, prev_tb[0])
                sgc = tick(pl, "vector", V, lambda e, c=c, sl=sl:
                           e.tensor_scalar(projst[:, c * 512:(c + 1) * 512],
                                           psW[sl][0:128, 0:512],
                                           boutc[:, c:c + 1], None,
                                           op0=AluOpType.add))
                psW_free[sl] = sgc
                pj_ticks.append(sgc)
            for tt4 in range(4):
                t = q4 * 4 + tt4
                sl = t % 2
                need("sync", V, xold_free[sl])
                ld = dma(pl, "sync", dO[sl], xoldt[:, sl * 1024:(sl + 1) * 1024],
                         src_res[t * 128:(t + 1) * 128, :])
                need("tensor", V, pj_ticks[-1])
                need("tensor", V, prev_vn[0])
                def _tb(e, tt4=tt4):
                    q = None
                    for c in range(8):
                        q = e.matmul(psR[:, c * 128:(c + 1) * 128],
                                     projst[:, c * 512 + tt4 * 128:
                                            c * 512 + (tt4 + 1) * 128],
                                     identB[:, :], start=True, stop=True)
                    return q
                tg = tick(pl, "tensor", T, _tb)
                prev_tb[0] = tg
                need("vector", T, tg)
                need("vector", dO[sl], ld)
                need("vector", dN[sl], xnew_free[sl])
                vn = tick(pl, "vector", V, lambda e, sl=sl:
                          e.tensor_tensor(xnewt[:, sl * 1024:(sl + 1) * 1024],
                                          xoldt[:, sl * 1024:(sl + 1) * 1024],
                                          psR[:, :], op=AluOpType.add))
                xold_free[sl] = vn
                prev_vn[0] = vn
                need("sync", V, vn)
                xnew_free[sl] = dma(pl, "sync", dN[sl],
                                    dst_res[t * 128:(t + 1) * 128, :],
                                    xnewt[:, sl * 1024:(sl + 1) * 1024])
        bar()
        CUR[0] = sb_o
        CURP[0] = ps_o
        pc.close()

    if n_sub >= 1:
        phase_A(yout if n_sub == 1 else xres)
    if n_sub >= 2:
        phase_ffn(xres, yout if n_sub == 2 else xres)
    if n_sub >= 3:
        phase_C(xres, yout if n_sub == 3 else xres)
    if n_sub >= 4:
        phase_ffn(xres, yout)

    wait(pl, "sync", dW, dW.n)
    wait(pl, "gpsimd", dG, dG.n)
    wait(pl, "scalar", dS, dS.n)

    with ExitStack() as sem_stack:
        for s_ in ALLSEMS:
            s_.h = sem_stack.enter_context(nc.semaphore(s_.name))
        with nc.Block() as blk:
            blk.sync(lambda e: [f(e) for f in pl.ev["sync"]])
            blk.scalar(lambda e: [f(e) for f in pl.ev["scalar"]])
            blk.vector(lambda e: [f(e) for f in pl.ev["vector"]])
            blk.tensor(lambda e: [f(e) for f in pl.ev["tensor"]])
            blk.gpsimd(lambda e: [f(e) for f in pl.ev["gpsimd"]])
    es.close()
    return nc


def kernel(**inputs):
    f32 = lambda k: np.ascontiguousarray(np.asarray(inputs[k], np.float32))
    xin = f32("L0_attn_a_x")
    common = {
        "wdq": f32("L0_attn_a_W_dq"), "wuq": f32("L0_attn_a_W_uq"),
        "wdkv": f32("L0_attn_a_W_dkv"), "wuk": f32("L0_attn_a_W_uk"),
        "wuv": f32("L0_attn_a_W_uv"), "wo": f32("L0_attn_a_W_o"),
        "wqs": f32("L0_attn_b_w_qs"), "wks": f32("L0_attn_b_w_ks"),
        "wvs": f32("L0_attn_b_w_vs"), "wout": f32("L0_attn_b_w_out"),
        "wrel": f32("L0_attn_b_w_rel_pos_k"),
        "rebm": f32("L0_attn_b_rel_embed_bias").reshape(HB, DB),
        "rpbm": f32("L0_attn_b_rel_pos_bias").reshape(HB, DB),
        "boutv": f32("L0_attn_b_b_out"),
        "wred": f32("L0_ffn_w_red"), "wgm": f32("L0_ffn_wg"),
        "W1": f32("L0_ffn_W1"), "b1m": f32("L0_ffn_b1"),
        "W2": f32("L0_ffn_W2"), "b2m": f32("L0_ffn_b2"),
        "identc": np.eye(128, dtype=np.float32),
    }
    pos = [_pos_staged(0), _pos_staged(1)]
    msk = [_masks_staged(0), _masks_staged(1)]
    in_maps = []
    for c in range(8):
        b, par = c // 2, c % 2
        xb = xin[b]
        m = dict(common)
        m["x_own"] = np.ascontiguousarray(
            xb.reshape(NKB, 128, D)[par::2].reshape(SQ, D))
        m["x_full"] = xb
        m["posc"] = pos[par]
        m["maskc"] = msk[par]
        in_maps.append(m)
    nc = build_program()
    res = run_bass_kernel_spmd(nc, in_maps, core_ids=list(range(8)))
    out = np.zeros((B, S, D), np.float32)
    for c in range(8):
        b, par = c // 2, c % 2
        yb = np.asarray(res.results[c]["yout"]).reshape(NSLOT, 128, D)
        out[b].reshape(NKB, 128, D)[par::2] = yb
    return out
